# version 25
# baseline (speedup 1.0000x reference)
import numpy as np
import concourse.bass as bass
import concourse.mybir as mybir
from concourse.bass_utils import run_bass_kernel_spmd
from contextlib import ExitStack

F32 = mybir.dt.float32
BF16 = mybir.dt.bfloat16
AF = mybir.ActivationFunctionType
ALU = mybir.AluOpType

ENGINES = ("sync", "tensor", "vector", "scalar", "gpsimd")

B_PER_CORE = 2
T = 2048
C = 1024
NKC = 8
TT = 256
NTT = (B_PER_CORE * T) // TT
MASKVAL = -30000.0
EPS = 1e-5


class _Op:
    __slots__ = ("engine", "fn", "deps", "need", "dma", "val", "lane", "lval")

    def __init__(self, engine, fn, deps, dma):
        self.engine, self.fn, self.deps, self.dma = engine, fn, deps, dma
        self.need = False
        self.val = 0
        self.lane = -1
        self.lval = 0


class Buf:
    __slots__ = ("w", "rs")

    def __init__(self):
        self.w = None
        self.rs = []


class Sched:
    def __init__(self):
        self.ops = {e: [] for e in ENGINES}

    def do(self, engine, fn, reads=(), writes=(), deps=(), dma=False):
        d = [x for x in deps if x is not None]
        for b in reads:
            if b.w is not None:
                d.append(b.w)
        for b in writes:
            if b.w is not None:
                d.append(b.w)
            d.extend(b.rs)
        op = _Op(engine, fn, d, dma)
        self.ops[engine].append(op)
        for b in reads:
            b.rs.append(op)
        for b in writes:
            b.w = op
            b.rs = []
        return op

    NLANES = 12

    def finalize(self, nc, es, paranoid=False, par_eng=()):
        for e in ENGINES:
            lane_uses = [0] * self.NLANES
            nxt = 0
            for o in self.ops[e]:
                if o.dma:
                    o.need = True
                    o.lane = nxt
                    lane_uses[nxt] += 1
                    o.lval = 16 * lane_uses[nxt]
                    nxt = (nxt + 1) % self.NLANES
        for e in ENGINES:
            for o in self.ops[e]:
                for dp in o.deps:
                    if (dp.engine != o.engine
                            or dp.engine in ("gpsimd", "vector")
                            or paranoid or dp.engine in par_eng):
                        dp.need = True
        sems = {e: es.enter_context(nc.semaphore("ch_" + e)) for e in ENGINES}
        lanes = [es.enter_context(nc.semaphore("lane%d" % i))
                 for i in range(self.NLANES)]
        for e in ENGINES:
            c = 0
            for o in self.ops[e]:
                if o.dma:
                    continue
                if o.need:
                    c += 1
                o.val = c
        block = es.enter_context(nc.Block())
        for e in ENGINES:
            ops = self.ops[e]
            if not ops:
                continue

            def body(eng, ops=ops, e=e):
                hw = {}
                def w(sem, v):
                    if v > hw.get(sem.num, 0):
                        eng.wait_ge(sem, v)
                        hw[sem.num] = v
                for o in ops:
                    for dp in o.deps:
                        if dp.dma:
                            w(lanes[dp.lane], dp.lval)
                        elif (dp.engine == e
                                and e not in ("gpsimd", "vector")
                                and not paranoid and e not in par_eng):
                            continue
                        else:
                            w(sems[dp.engine], dp.val)
                    if o.dma and o.lval > 16:
                        w(lanes[o.lane], o.lval - 16)
                    inst = o.fn(eng) if o.fn is not None else None
                    if o.dma:
                        inst.then_inc(lanes[o.lane], 16)
                    elif o.need:
                        inst.then_inc(sems[e], 1)

            getattr(block, e)(body)


def build_program(taps=None, ntt=NTT, paranoid=False, stop=None, par_eng=()):
    taps = taps or {}
    nc = bass.Bass("TRN2", target_bir_lowering=False, debug=False,
                   detect_race_conditions=paranoid)
    x_d = nc.dram_tensor("x", [B_PER_CORE, T, C], F32, kind="ExternalInput").ap()
    w_d = {}
    for name in ("wq", "wk", "wv", "wo", "wf"):
        w_d[name] = nc.dram_tensor(name, [C, C], F32, kind="ExternalInput").ap()
    disha_d = nc.dram_tensor("disha", [64, C], F32, kind="ExternalInput").ap()
    bias_d = nc.dram_tensor("bias", [1, 4 * C], F32, kind="ExternalInput").ap()
    biasT_d = nc.dram_tensor("biasT", [128, 32], F32, kind="ExternalInput").ap()
    out_d = nc.dram_tensor("out", [B_PER_CORE, T, C], F32, kind="ExternalOutput").ap()
    tap_d = {}
    for tname, spec in {
        "q_fm": [128, 8, 256], "k_fm": [128, 8, 384], "v_tm": [128, 3, 1024],
        "attn": [128, 8, 256], "y": [128, 2, 1024], "pool": [64, 256],
        "z1": [128, 2, 1024],
    }.items():
        if tname in taps:
            dt = BF16 if tname in ("q_fm", "k_fm", "v_tm", "attn", "pool") else F32
            tap_d[tname] = nc.dram_tensor("tap_" + tname, spec, dt,
                                          kind="ExternalOutput").ap()

    S = Sched()
    es = ExitStack()
    with es:
        E = es.enter_context

        def sb(name, shape, dt):
            return E(nc.sbuf_tensor(name, shape, dt))

        wq_b = sb("wq_b", [128, 8, 1024], BF16)
        wk_b = sb("wk_b", [128, 8, 1024], BF16)
        wv_b = sb("wv_b", [128, 8, 1024], BF16)
        wo_b = sb("wo_b", [128, 8, 1024], BF16)
        wfT_b = sb("wfT_b", [128, 8, 1024], BF16)
        disha_b = sb("disha_b", [64, 1024], BF16)
        brow_bf = sb("brow_bf", [1, 4096], BF16)
        ones_t = sb("ones_t", [1, 256], BF16)
        eps_t = sb("eps_t", [128, 1], F32)
        biasT = sb("biasT_sb", [128, 40], F32)
        mask4 = sb("mask4", [128, 1024], BF16)
        maskf4 = sb("maskf4", [128, 1024], BF16)
        identb = sb("identb", [128, 128], BF16)
        gmat_b = sb("gmat_b", [128, 64], BF16)
        stage = sb("stage", [128, 1024], F32)
        identf = stage[:, 0:128]
        gtmp = stage[:, 128:256]
        maskstage = stage[:, 256:512]
        x_tm = [sb("x_tm0", [128, 2, 1024], F32), sb("x_tm1", [128, 2, 1024], F32)]
        x_bf = sb("x_bf", [128, 2048], BF16)
        x_fm = sb("x_fm", [128, 8, 256], BF16)
        q_fm = sb("q_fm", [128, 8, 256], BF16)
        k_fm = sb("k_fm", [128, 8, 384], BF16)
        v_tm = sb("v_tm", [128, 3, 1024], BF16)
        NWP = 3
        NWPB = 4
        p32 = [sb("p32_%d" % i, [128, 1024], F32) for i in range(NWP)]
        pbf = [sb("pbf_%d" % i, [128, 1024], BF16) for i in range(NWPB)]
        ptb = [sb("ptb_%d" % i, [128, 1024], BF16) for i in range(NWPB)]
        den = [sb("den_%d" % i, [128, 4], F32) for i in range(NWP)]
        rec = [sb("rec_%d" % i, [128, 4], F32) for i in range(NWP)]
        attn_fm = sb("attn_fm", [128, 8, 256], BF16)
        z1 = sb("z1", [128, 2, 1024], F32)
        y_t = sb("y_t", [128, 2, 1024], F32)
        y_bf = sb("y_bf", [128, 2048], BF16)
        y_fm = sb("y_fm", [128, 8, 256], BF16)
        zz = stage
        pool_b = sb("pool_b", [64, 256], BF16)
        outb = sb("outb", [128, 2, 1024], F32)
        sums = sb("sums", [128, 2], F32)
        sumsq = sb("sumsq", [128, 2], F32)
        mean_t = sb("mean_t", [128, 2], F32)
        msq = sb("msq", [128, 2], F32)
        var_t = sb("var_t", [128, 2], F32)
        lnv = sb("lnv", [128, 2], F32)
        rs_t = sb("rs_t", [128, 2], F32)
        sb_t = sb("sb_t", [128, 2], F32)
        ps_sc = [E(nc.psum_tensor("ps_sc0", [128, 1024], F32)),
                 E(nc.psum_tensor("ps_sc1", [128, 1024], F32))]
        ps_tr = [E(nc.psum_tensor("ps_tr0", [128, 1024], BF16)),
                 E(nc.psum_tensor("ps_tr1", [128, 1024], BF16))]
        ps_bg = E(nc.psum_tensor("ps_bg", [128, 1024], F32))

        Bw = {n: [Buf() for _ in range(NKC)] for n in ("wq", "wk", "wv", "wo")}
        Bw["wf"] = [Buf()] * NKC
        Bdisha, Bones, Beps, BbiasT = Buf(), Buf(), Buf(), Buf()
        Bbrowbf = Buf()
        Bmask4, Bmaskf4, Bidentb = Buf(), Buf(), Buf()
        Bgmat, Bstage = Buf(), Buf()
        Bidentf = Bgtmp = Bstage
        Bxtm = [[Buf(), Buf()], [Buf(), Buf()]]
        Bxbf, Bxfm = Buf(), Buf()
        Bqfm, Bkfm, Bvtm = Buf(), Buf(), Buf()
        Bp32 = [Buf() for _ in range(NWP)]
        Bpbf = [Buf() for _ in range(NWPB)]
        Bptb = [Buf() for _ in range(NWPB)]
        Bden = [Buf() for _ in range(NWP)]
        Brec = [Buf() for _ in range(NWP)]
        Battn = [Buf(), Buf()]
        Bz1 = [Buf(), Buf()]
        By = [Buf(), Buf()]
        Bybf = [Buf(), Buf()]
        Byfm = Buf()
        Bzz, Bpool = Bstage, Buf()
        Boutb = [Buf(), Buf()]
        Bsums, Bsumsq, Bmean, Bmsq = Buf(), Buf(), Buf(), Buf()
        Bvar, Blnv, Brs, Bsb = Buf(), Buf(), Buf(), Buf()
        Bpsc = [Buf(), Buf()]
        Bpstr = [Buf(), Buf()]
        Bbg = [Buf(), Buf()]
        psf = [ps_bg, ps_sc[0], ps_sc[1]]
        Bps = [[Buf(), Buf()] for _ in range(3)]
        _rot = [0]

        def grab():
            i = _rot[0] % 3
            _rot[0] += 1
            return psf[i], Bps[i]

        xdma = {}

        def load_x(g):
            b, tt = g // 8, g % 8
            t0 = tt * TT
            par = g % 2
            ops = []
            for jt in range(2):
                ops.append(S.do(
                    "sync",
                    lambda e, b=b, t0=t0, jt=jt, par=par: e.dma_start(
                        out=x_tm[par][:, jt, :],
                        in_=x_d[b, t0 + 128 * jt:t0 + 128 * (jt + 1), :]),
                    writes=[Bxtm[par][jt]], dma=True))
            xdma[g] = ops

        load_x(0)

        S.do("vector", lambda e: e.memset(ones_t[:], 1.0), writes=[Bones])
        S.do("vector", lambda e: e.memset(eps_t[:], EPS), writes=[Beps])
        S.do("sync", lambda e: e.dma_start(out=biasT[:, 0:32], in_=biasT_d[:]),
             writes=[BbiasT], dma=True)
        S.do("vector", lambda e: e.tensor_scalar_mul(
            biasT[:, 32:40], biasT[:, 8:16], 0.125), writes=[BbiasT])
        for kq in range(4):
            S.do("sync", lambda e, kq=kq: e.dma_start(
                out=stage[0:1, :], in_=bias_d[0:1, 1024 * kq:1024 * (kq + 1)]),
                writes=[Bstage], dma=True)
            S.do("vector", lambda e, kq=kq: e.tensor_copy(
                brow_bf[0:1, 1024 * kq:1024 * (kq + 1)], stage[0:1, :]),
                reads=[Bstage], writes=[Bbrowbf])

        S.do("gpsimd", lambda e: e.memset(identf[:, :], 1.0), writes=[Bidentf])
        S.do("gpsimd", lambda e: e.affine_select(
            identf[:, :], identf[:, :], [[1, 128]], ALU.is_equal, 0.0,
            base=0, channel_multiplier=-1), writes=[Bidentf])
        S.do("vector", lambda e: e.tensor_copy(identb[:], identf[:, :]),
             reads=[Bidentf], writes=[Bidentb])
        S.do("gpsimd", lambda e: e.memset(gtmp[:], 1.0), writes=[Bgtmp])
        S.do("gpsimd", lambda e: e.affine_select(
            gtmp[:, 0:64], gtmp[:, 0:64], [[1, 64]], ALU.is_equal, 0.0,
            base=0, channel_multiplier=-1), writes=[Bgtmp])
        S.do("gpsimd", lambda e: e.affine_select(
            gtmp[:, 64:128], gtmp[:, 64:128], [[1, 64]], ALU.is_equal, 0.0,
            base=64, channel_multiplier=-1), writes=[Bgtmp])
        S.do("vector", lambda e: e.tensor_add(gtmp[:, 0:64], gtmp[:, 0:64],
                                              gtmp[:, 64:128]), writes=[Bgtmp])
        S.do("vector", lambda e: e.tensor_copy(gmat_b[:], gtmp[:, 0:64]),
             reads=[Bgtmp], writes=[Bgmat])
        S.do("gpsimd", lambda e: e.memset(maskstage[:, :], 0.0), writes=[Bstage])
        S.do("gpsimd", lambda e: e.affine_select(
            maskstage[:, :], maskstage[:, :], [[1, 256]], ALU.is_ge, MASKVAL,
            base=0, channel_multiplier=-1), writes=[Bstage])
        S.do("gpsimd", lambda e: e.affine_select(
            maskstage[:, :], maskstage[:, :], [[-1, 256]], ALU.is_ge, MASKVAL,
            base=128, channel_multiplier=1), writes=[Bstage])
        for i in range(4):
            S.do("vector", lambda e, i=i: e.tensor_copy(
                mask4[:, 256 * i:256 * (i + 1)], maskstage[:, :]),
                reads=[Bstage], writes=[Bmask4])
        S.do("vector", lambda e: e.memset(maskf4[:], MASKVAL), writes=[Bmaskf4])
        for i in range(4):
            S.do("vector", lambda e, i=i: e.tensor_copy(
                maskf4[:, 256 * i + 128:256 * i + 256], mask4[:, 128:256]),
                reads=[Bmask4], writes=[Bmaskf4])

        S.do("sync", lambda e: e.dma_start(out=stage[0:64, :], in_=disha_d[:]),
             writes=[Bstage], dma=True)
        S.do("vector", lambda e: e.tensor_copy(disha_b[:], stage[0:64, :]),
             reads=[Bstage], writes=[Bdisha])

        stg = [(stage, Bstage)]
        sidx = 0
        for name, wb in (("wq", wq_b), ("wk", wk_b), ("wv", wv_b), ("wo", wo_b)):
            for kc in range(NKC):
                st_t, st_b = stg[0]
                sidx += 1
                S.do("sync", lambda e, name=name, kc=kc, st_t=st_t: e.dma_start(
                    out=st_t[:], in_=w_d[name][128 * kc:128 * (kc + 1), :]),
                    writes=[st_b], dma=True)
                S.do("vector", lambda e, wb=wb, kc=kc, st_t=st_t: e.tensor_copy(
                    wb[:, kc, :], st_t[:]), reads=[st_b], writes=[Bw[name][kc]])
        out_dmas = []
        wave_ctr = [0]

        class _StopBuild(Exception):
            pass


        def emit_xcast(g):
            par = g % 2
            S.do("vector", lambda e, par=par: e.tensor_copy(
                x_bf[:], x_tm[par][:]),
                reads=[Bxtm[par][0], Bxtm[par][1]], writes=[Bxbf])

        def emit_xtr(g, jt):
            for kc in range(NKC):
                S.do("tensor", lambda e, jt=jt, kc=kc: e.transpose(
                    ps_tr[jt][:, 128 * kc:128 * (kc + 1)],
                    x_bf[:, 1024 * jt + 128 * kc:1024 * jt + 128 * (kc + 1)],
                    identb[:]), reads=[Bxbf, Bidentb], writes=[Bpstr[jt]])
            S.do("vector", lambda e, jt=jt: e.tensor_copy(
                x_fm[:, :, 128 * jt:128 * (jt + 1)], ps_tr[jt][:]),
                reads=[Bpstr[jt]], writes=[Bxfm])

        def emit_xpipe(g):
            b, tt = g // 8, g % 8
            par = g % 2
            if g + 1 < ntt:
                load_x(g + 1)
            if tt == 0:
                S.do("vector", lambda e: e.memset(k_fm[:, :, 0:128], 0.0),
                     writes=[Bkfm])
                S.do("vector", lambda e: e.memset(v_tm[:, 0, :], 0.0),
                     writes=[Bvtm])
            for name, wb, bi in (("wq", wq_b, 0), ("wk", wk_b, 1)):
                for half in range(2):
                    ps, Bp = grab()
                    for i in range(4):
                        ncx = 4 * half + i
                        reg = ps[:, 256 * i:256 * (i + 1)]
                        for kc in range(NKC):
                            S.do("tensor", lambda e, reg=reg, wb=wb, kc=kc,
                                 ncx=ncx: e.matmul(
                                     reg, wb[:, kc, 128 * ncx:128 * (ncx + 1)],
                                     x_fm[:, kc, :], start=(kc == 0),
                                     stop=(kc == NKC - 1)),
                                 reads=[Bw[name][kc], Bxfm], writes=[Bp[i // 2]])
                    for i in range(4):
                        ncx = 4 * half + i
                        if name == "wq":
                            S.do("scalar", lambda e, i=i, ncx=ncx, ps=ps:
                                 e.activation(
                                     q_fm[:, ncx, :],
                                     ps[:, 256 * i:256 * (i + 1)],
                                     AF.Identity,
                                     bias=biasT[:, ncx:ncx + 1], scale=1.0),
                                 reads=[Bp[i // 2], BbiasT], writes=[Bqfm])
                        else:
                            S.do("scalar", lambda e, i=i, ncx=ncx, ps=ps:
                                 e.activation(
                                     k_fm[:, ncx, 128:384],
                                     ps[:, 256 * i:256 * (i + 1)],
                                     AF.Identity,
                                     bias=biasT[:, 32 + ncx:33 + ncx],
                                     scale=0.125),
                                 reads=[Bp[i // 2], BbiasT], writes=[Bkfm])
            for jt in range(2):
                ps, Bp = grab()
                for nh in range(2):
                    reg = ps[:, 512 * nh:512 * (nh + 1)]
                    for kc in range(NKC):
                        S.do("tensor", lambda e, reg=reg, kc=kc, jt=jt, nh=nh:
                             e.matmul(reg,
                                      x_fm[:, kc, 128 * jt:128 * (jt + 1)],
                                      wv_b[:, kc, 512 * nh:512 * (nh + 1)],
                                      start=(kc == 0), stop=False),
                             reads=[Bw["wv"][kc], Bxfm], writes=[Bp[nh]])
                    S.do("tensor", lambda e, reg=reg, nh=nh: e.matmul(
                        reg, ones_t[:, 0:128],
                        brow_bf[0:1, 2048 + 512 * nh:2048 + 512 * (nh + 1)],
                        start=False, stop=True),
                        reads=[Bones, Bbrowbf], writes=[Bp[nh]])
                S.do("scalar", lambda e, jt=jt, ps=ps: e.activation(
                    v_tm[:, 1 + jt, :], ps[:], AF.Identity, bias=0.0, scale=1.0),
                    reads=[Bp[0], Bp[1]], writes=[Bvtm])
            if taps.get("q_fm") == g:
                S.do("sync", lambda e: e.dma_start(out=tap_d["q_fm"], in_=q_fm[:]),
                     reads=[Bqfm], dma=True)
                S.do("sync", lambda e: e.dma_start(out=tap_d["k_fm"], in_=k_fm[:]),
                     reads=[Bkfm], dma=True)
                S.do("sync", lambda e: e.dma_start(out=tap_d["v_tm"], in_=v_tm[:]),
                     reads=[Bvtm], dma=True)

        def _att_s1(g, w):
            first = (g % 8 == 0)
            qt, ww = w // 4, w % 4
            qs = 128 * qt
            sp = 1 + w % 2
            pairs = (2 * ww, 2 * ww + 1)
            mk = maskf4 if (first and qt == 0) else mask4
            mb = Bmaskf4 if (first and qt == 0) else Bmask4
            for pi, p2 in enumerate(pairs):
                for hp in range(2):
                    col = 512 * hp + 256 * pi
                    S.do("tensor", lambda e, sp=sp, col=col, hp=hp,
                         p2=p2, qs=qs: e.matmul(
                             psf[sp][:, col:col + 256],
                             q_fm[64 * hp:64 * (hp + 1), p2, qs:qs + 128],
                             k_fm[64 * hp:64 * (hp + 1), p2, qs:qs + 256],
                             start=True, stop=True,
                             tile_position=(64 * hp, 0)),
                         reads=[Bqfm, Bkfm], writes=[Bps[sp][hp]])
                for hp in range(2):
                    j = 2 * hp + pi
                    S.do("tensor", lambda e, sp=sp, j=j, mk=mk: e.matmul(
                        psf[sp][:, 256 * j:256 * (j + 1)], identb[:],
                        mk[:, 256 * j:256 * (j + 1)], start=False, stop=True,
                        skip_group_check=True),
                        reads=[mb, Bidentb], writes=[Bps[sp][j // 2]])

        def _att_s2(g, w):
            wp = w % NWP
            wb_ = w % NWPB
            sp = 1 + w % 2
            for j in range(4):
                S.do("scalar", lambda e, wp=wp, sp=sp, j=j: e.activation(
                    p32[wp][:, 256 * j:256 * (j + 1)],
                    psf[sp][:, 256 * j:256 * (j + 1)], AF.Exp,
                    bias=0.0, scale=1.0,
                    accum_out=den[wp][:, j:j + 1]),
                    reads=[Bps[sp][j // 2]],
                    writes=[Bp32[wp], Bden[wp]])
            S.do("vector", lambda e, wp=wp: e.reciprocal(
                rec[wp][:], den[wp][:]), reads=[Bden[wp]], writes=[Brec[wp]])
            for j in range(4):
                S.do("vector", lambda e, wp=wp, wb_=wb_, j=j: e.tensor_scalar_mul(
                    pbf[wb_][:, 256 * j:256 * (j + 1)],
                    p32[wp][:, 256 * j:256 * (j + 1)],
                    rec[wp][:, j:j + 1]),
                    reads=[Bp32[wp], Brec[wp]], writes=[Bpbf[wb_]])

        def _att_s3(g, w):
            wb_ = w % NWPB
            tp = w % 2
            for i in range(8):
                S.do("tensor", lambda e, wb_=wb_, tp=tp, i=i: e.transpose(
                    ps_tr[tp][:, 128 * i:128 * (i + 1)],
                    pbf[wb_][:, 128 * i:128 * (i + 1)], identb[:]),
                    reads=[Bpbf[wb_], Bidentb], writes=[Bpstr[tp]])
            S.do("vector", lambda e, wb_=wb_, tp=tp: e.tensor_copy(
                ptb[wb_][:], ps_tr[tp][:]), reads=[Bpstr[tp]],
                writes=[Bptb[wb_]])

        def _att_s4(g, w):
            wp = w % NWPB
            qt, ww = w // 4, w % 4
            qs = 128 * qt
            pairs = (2 * ww, 2 * ww + 1)
            for pi, p2 in enumerate(pairs):
                half = p2 // 4
                colq = 512 * half + 128 * (p2 % 4)
                for hp in range(2):
                    h = 2 * p2 + hp
                    j = 2 * hp + pi
                    for kb in range(2):
                        S.do("tensor", lambda e, wp=wp, hp=hp, h=h, j=j,
                             kb=kb, qt=qt, colq=colq: e.matmul(
                                 psf[0][64 * hp:64 * (hp + 1),
                                        colq:colq + 128],
                                 v_tm[:, qt + kb, 64 * h:64 * (h + 1)],
                                 ptb[wp][:, 256 * j + 128 * kb:
                                          256 * j + 128 * (kb + 1)],
                                 start=(kb == 0), stop=(kb == 1),
                                 tile_position=(0, 64 * hp)),
                             reads=[Bvtm, Bptb[wp]], writes=[Bps[0][half]])
            if ww in (1, 3):
                half = 0 if ww == 1 else 1
                S.do("scalar", lambda e, half=half, qs=qs, qt=qt: e.activation(
                    attn_fm[:, 4 * half:4 * (half + 1), qs:qs + 128],
                    psf[0][:, 512 * half:512 * (half + 1)],
                    AF.Identity, bias=0.0, scale=1.0),
                    reads=[Bps[0][half]], writes=[Battn[qt]])

        STAGGER = 1

        def emit_attention_all(g):
            for s in range(12):
                if s < 8:
                    _att_s1(g, s)
                if 1 <= s <= 8:
                    _att_s2(g, s - 1)
                if s in (1, 2) and g + 1 < ntt:
                    emit_xtr(g + 1, s - 1)
                if 3 <= s <= 10:
                    _att_s3(g, s - 3)
                if 4 <= s <= 11:
                    _att_s4(g, s - 4)

        def emit_wo_mm(g, jt):
            ps, Bp = grab()
            for nh in range(2):
                reg = ps[:, 512 * nh:512 * (nh + 1)]
                for kc in range(NKC):
                    S.do("tensor", lambda e, reg=reg, kc=kc, jt=jt, nh=nh:
                         e.matmul(reg,
                                  attn_fm[:, kc, 128 * jt:128 * (jt + 1)],
                                  wo_b[:, kc, 512 * nh:512 * (nh + 1)],
                                  start=(kc == 0), stop=False),
                         reads=[Battn[jt], Bw["wo"][kc]], writes=[Bp[nh]])
                S.do("tensor", lambda e, reg=reg, nh=nh: e.matmul(
                    reg, ones_t[:, 0:128],
                    brow_bf[0:1, 3072 + 512 * nh:3072 + 512 * (nh + 1)],
                    start=False, stop=True),
                    reads=[Bones, Bbrowbf], writes=[Bp[nh]])
            return ps, Bp

        def emit_ln1_y(g, jt, ps, Bp):
            par = g % 2
            _ln(S, nc, "vector", jt, par, ps, Bp, x_tm, Bxtm, z1, Bz1,
                y_t, By, zz, Bzz, sums, Bsums, sumsq, Bsumsq, mean_t, Bmean,
                msq, Bmsq, var_t, Bvar, lnv, Blnv, rs_t, Brs, sb_t, Bsb,
                eps_t, Beps, resid_src="x")
            S.do("vector", lambda e, jt=jt: e.tensor_copy(
                y_bf[:, 1024 * jt:1024 * (jt + 1)], y_t[:, jt, :]),
                reads=[By[jt]], writes=[Bybf[jt]])

        def emit_ytr(g, jt):
            for kc in range(NKC):
                S.do("tensor", lambda e, jt=jt, kc=kc: e.transpose(
                    ps_tr[jt][:, 128 * kc:128 * (kc + 1)],
                    y_bf[:, 1024 * jt + 128 * kc:1024 * jt + 128 * (kc + 1)],
                    identb[:]), reads=[Bybf[jt], Bidentb], writes=[Bpstr[jt]])
            S.do("vector", lambda e, jt=jt: e.tensor_copy(
                y_fm[:, :, 128 * jt:128 * (jt + 1)], ps_tr[jt][:]),
                reads=[Bpstr[jt]], writes=[Byfm])

        def emit_ffn_out(g, jt):
            b, tt = g // 8, g % 8
            t0 = tt * TT
            par = g % 2
            ps, Bp = grab()
            for nh in range(2):
                reg = ps[:, 512 * nh:512 * (nh + 1)]
                for kc in range(NKC):
                    S.do("tensor", lambda e, reg=reg, kc=kc, jt=jt, nh=nh:
                         e.matmul(reg,
                                  y_fm[:, kc, 128 * jt:128 * (jt + 1)],
                                  wfT_b[:, kc, 512 * nh:512 * (nh + 1)],
                                  start=(kc == 0), stop=False),
                         reads=[Byfm, Bw["wf"][kc]], writes=[Bp[nh]])
                S.do("tensor", lambda e, reg=reg, jt=jt, nh=nh: e.matmul(
                    reg, pool_b[0:64, 128 * jt:128 * (jt + 1)],
                    disha_b[0:64, 512 * nh:512 * (nh + 1)],
                    start=False, stop=True),
                    reads=[Bpool, Bdisha], writes=[Bp[nh]])
            _ln(S, nc, "vector", jt, par, ps, Bp, None, None, z1, Bz1,
                outb, Boutb, zz, Bzz, sums, Bsums, sumsq, Bsumsq,
                mean_t, Bmean, msq, Bmsq, var_t, Bvar, lnv, Blnv, rs_t, Brs,
                sb_t, Bsb, eps_t, Beps, resid_src="y", y_t=y_t, By=By)
            op = S.do("sync", lambda e, b=b, t0=t0, jt=jt:
                      e.dma_start(
                          out=out_d[b, t0 + 128 * jt:t0 + 128 * (jt + 1), :],
                          in_=outb[:, jt, :]),
                      reads=[Boutb[jt]], dma=True)
            out_dmas.append(op)

        emit_xcast(0)
        emit_xtr(0, 0)
        emit_xtr(0, 1)
        emit_xpipe(0)
        for ncx in range(NKC):
            st_t, st_b = stg[0]
            sidx += 1
            S.do("sync", lambda e, ncx=ncx, st_t=st_t: e.dma_start(
                out=st_t[:], in_=w_d["wf"][128 * ncx:128 * (ncx + 1), :]),
                writes=[st_b], dma=True)
            S.do("vector", lambda e, st_t=st_t: e.tensor_copy(
                pbf[0][:], st_t[:]),
                 reads=[st_b], writes=[Bpbf[0]])
            for kc in range(NKC):
                S.do("tensor", lambda e, kc=kc: e.transpose(
                    ps_tr[0][:, 128 * kc:128 * (kc + 1)],
                    pbf[0][:, 128 * kc:128 * (kc + 1)], identb[:]),
                    reads=[Bpbf[0], Bidentb], writes=[Bpstr[0]])
            S.do("scalar", lambda e, ncx=ncx: e.activation(
                wfT_b[:, :, 128 * ncx:128 * (ncx + 1)], ps_tr[0][:],
                AF.Identity, bias=0.0, scale=1.0),
                reads=[Bpstr[0]], writes=[Bw["wf"][0]])


        for g in range(ntt):
            tt = g % 8
            if g + 1 < ntt:
                emit_xcast(g + 1)
            emit_attention_all(g)
            ps0, Bp0 = emit_wo_mm(g, 0)
            ps1, Bp1 = emit_wo_mm(g, 1)
            emit_ln1_y(g, 0, ps0, Bp0)
            emit_ln1_y(g, 1, ps1, Bp1)
            if taps.get("attn") == g:
                S.do("sync", lambda e: e.dma_start(out=tap_d["attn"],
                                                   in_=attn_fm[:]),
                     reads=[Battn[0], Battn[1]], dma=True)
            if taps.get("y") == g:
                S.do("sync", lambda e: e.dma_start(out=tap_d["y"], in_=y_t[:]),
                     reads=[By[0], By[1]], dma=True)
            if taps.get("z1") == g:
                S.do("sync", lambda e: e.dma_start(out=tap_d["z1"], in_=z1[:]),
                     reads=[Bz1[0], Bz1[1]], dma=True)
            if tt < 7:
                S.do("vector", lambda e: e.tensor_copy(
                    k_fm[:, :, 0:128], k_fm[:, :, 256:384]), writes=[Bkfm])
                S.do("vector", lambda e: e.tensor_copy(
                    v_tm[:, 0, :], v_tm[:, 2, :]), writes=[Bvtm])
            if g + 1 < ntt:
                emit_xpipe(g + 1)
            emit_ytr(g, 0)
            emit_ytr(g, 1)
            ps, Bp = grab()
            for kc in range(NKC):
                S.do("tensor", lambda e, kc=kc, ps=ps: e.matmul(
                    ps[0:64, 0:256], gmat_b[:], y_fm[:, kc, :],
                    start=(kc == 0), stop=(kc == NKC - 1)),
                    reads=[Bgmat, Byfm], writes=[Bp[0]])
            S.do("scalar", lambda e, ps=ps: e.activation(
                pool_b[:], ps[0:64, 0:256], AF.Identity, bias=0.0, scale=1.0),
                reads=[Bp[0]], writes=[Bpool])
            if taps.get("pool") == g:
                S.do("sync", lambda e: e.dma_start(out=tap_d["pool"],
                                                   in_=pool_b[:]),
                     reads=[Bpool], dma=True)
            emit_ffn_out(g, 0)
            emit_ffn_out(g, 1)
        S.do("sync", None, deps=out_dmas)

        S.finalize(nc, es, paranoid, par_eng)
    return nc


def _ln(S, nc, veng, jt, par, ps_bg, Bbg, x_tm, Bxtm, z1, Bz1, dst, Bdst,
        zz, Bzz, sums, Bsums, sumsq, Bsumsq, mean_t, Bmean, msq, Bmsq,
        var_t, Bvar, lnv, Blnv, rs_t, Brs, sb_t, Bsb, eps_t, Beps,
        resid_src="x", y_t=None, By=None):
    if resid_src == "x":
        resid = x_tm[par][:, jt, :]
        Bres = Bxtm[par][jt]
    else:
        resid = y_t[:, jt, :]
        Bres = By[jt]
    S.do("vector", lambda e: e.scalar_tensor_tensor(
        z1[:, jt, :], ps_bg[:], 1.0, resid, ALU.mult, ALU.add,
        accum_out=sums[:, jt:jt + 1]),
        reads=[Bbg[0], Bbg[1], Bres], writes=[Bz1[jt], Bsums])
    S.do("vector", lambda e: e.scalar_tensor_tensor(
        zz[:], z1[:, jt, :], 1.0, z1[:, jt, :], ALU.mult, ALU.mult,
        accum_out=sumsq[:, jt:jt + 1]),
        reads=[Bz1[jt]], writes=[Bzz, Bsumsq])
    S.do("vector", lambda e: e.tensor_scalar_mul(
        mean_t[:, jt:jt + 1], sums[:, jt:jt + 1], 1.0 / C),
        reads=[Bsums], writes=[Bmean])
    S.do("vector", lambda e: e.tensor_mul(
        msq[:, jt:jt + 1], mean_t[:, jt:jt + 1], mean_t[:, jt:jt + 1]),
        reads=[Bmean], writes=[Bmsq])
    S.do("vector", lambda e: e.scalar_tensor_tensor(
        var_t[:, jt:jt + 1], sumsq[:, jt:jt + 1], 1.0 / C, msq[:, jt:jt + 1],
        ALU.mult, ALU.subtract),
        reads=[Bsumsq, Bmsq], writes=[Bvar])
    S.do("scalar", lambda e: e.activation(
        lnv[:, jt:jt + 1], var_t[:, jt:jt + 1], AF.Ln, bias=eps_t[:], scale=1.0),
        reads=[Bvar, Beps], writes=[Blnv])
    S.do("scalar", lambda e: e.activation(
        rs_t[:, jt:jt + 1], lnv[:, jt:jt + 1], AF.Exp, bias=0.0, scale=-0.5),
        reads=[Blnv], writes=[Brs])
    S.do("vector", lambda e: e.tensor_scalar(
        sb_t[:, jt:jt + 1], mean_t[:, jt:jt + 1], rs_t[:, jt:jt + 1], -1.0,
        ALU.mult, ALU.mult),
        reads=[Bmean, Brs], writes=[Bsb])
    S.do("scalar", lambda e: e.activation(
        dst[:, jt, :], z1[:, jt, :], AF.Identity,
        bias=sb_t[:, jt:jt + 1], scale=rs_t[:, jt:jt + 1]),
        reads=[Bz1[jt], Brs, Bsb], writes=[Bdst[jt]])


_NC_CACHE = {}


def _get_nc(taps_key=None):
    if taps_key not in _NC_CACHE:
        _NC_CACHE[taps_key] = build_program(dict(taps_key) if taps_key else None)
    return _NC_CACHE[taps_key]


def kernel(L0_attn_x, L0_attn_Wq, L0_attn_bq, L0_attn_Wk, L0_attn_bk,
           L0_attn_Wv, L0_attn_bv, L0_attn_Wo, L0_attn_bo,
           L0_ffn_weight, L0_ffn_disha):
    x = np.ascontiguousarray(np.asarray(L0_attn_x))
    biases = np.ascontiguousarray(np.stack([
        np.asarray(L0_attn_bq), np.asarray(L0_attn_bk),
        np.asarray(L0_attn_bv), np.asarray(L0_attn_bo)]).reshape(1, 4 * 1024))
    biasT = np.ascontiguousarray(np.stack(
        [np.asarray(b).reshape(8, 128).T for b in
         (L0_attn_bq, L0_attn_bk, L0_attn_bv, L0_attn_bo)],
        axis=1).reshape(128, 32))
    common = {
        "biasT": biasT,
        "wq": np.ascontiguousarray(np.asarray(L0_attn_Wq)),
        "wk": np.ascontiguousarray(np.asarray(L0_attn_Wk)),
        "wv": np.ascontiguousarray(np.asarray(L0_attn_Wv)),
        "wo": np.ascontiguousarray(np.asarray(L0_attn_Wo)),
        "wf": np.ascontiguousarray(np.asarray(L0_ffn_weight)),
        "disha": np.ascontiguousarray(np.asarray(L0_ffn_disha)),
        "bias": biases,
    }
    in_maps = [dict(common, x=np.ascontiguousarray(x[2 * c:2 * c + 2]))
               for c in range(8)]
    nc = _get_nc()
    res = run_bass_kernel_spmd(nc, in_maps, core_ids=list(range(8)))
    out = np.concatenate([res.results[c]["out"] for c in range(8)], axis=0)
    return out.astype(np.float32, copy=False)


# version 26
# speedup vs baseline: 1.2475x; 1.2475x over previous
import numpy as np
import concourse.bass as bass
import concourse.mybir as mybir
from concourse.bass_utils import run_bass_kernel_spmd
from contextlib import ExitStack

F32 = mybir.dt.float32
BF16 = mybir.dt.bfloat16
AF = mybir.ActivationFunctionType
ALU = mybir.AluOpType

ENGINES = ("sync", "tensor", "vector", "scalar", "gpsimd")

B_PER_CORE = 2
T = 2048
C = 1024
NKC = 8
TT = 256
NTT = (B_PER_CORE * T) // TT
MASKVAL = -30000.0
EPS = 1e-5


class _Op:
    __slots__ = ("engine", "fn", "deps", "need", "dma", "val", "lane", "lval")

    def __init__(self, engine, fn, deps, dma):
        self.engine, self.fn, self.deps, self.dma = engine, fn, deps, dma
        self.need = False
        self.val = 0
        self.lane = -1
        self.lval = 0


class Buf:
    __slots__ = ("w", "rs")

    def __init__(self):
        self.w = None
        self.rs = []


class Sched:
    def __init__(self):
        self.ops = {e: [] for e in ENGINES}

    def do(self, engine, fn, reads=(), writes=(), deps=(), dma=False):
        d = [x for x in deps if x is not None]
        for b in reads:
            if b.w is not None:
                d.append(b.w)
        for b in writes:
            if b.w is not None:
                d.append(b.w)
            d.extend(b.rs)
        op = _Op(engine, fn, d, dma)
        self.ops[engine].append(op)
        for b in reads:
            b.rs.append(op)
        for b in writes:
            b.w = op
            b.rs = []
        return op

    NLANES = 12

    def finalize(self, nc, es, paranoid=False, par_eng=()):
        for e in ENGINES:
            lane_uses = [0] * self.NLANES
            nxt = 0
            for o in self.ops[e]:
                if o.dma:
                    o.need = True
                    o.lane = nxt
                    lane_uses[nxt] += 1
                    o.lval = 16 * lane_uses[nxt]
                    nxt = (nxt + 1) % self.NLANES
        for e in ENGINES:
            for o in self.ops[e]:
                for dp in o.deps:
                    if (dp.engine != o.engine
                            or dp.engine in ("gpsimd", "vector")
                            or paranoid or dp.engine in par_eng):
                        dp.need = True
        sems = {e: es.enter_context(nc.semaphore("ch_" + e)) for e in ENGINES}
        lanes = [es.enter_context(nc.semaphore("lane%d" % i))
                 for i in range(self.NLANES)]
        for e in ENGINES:
            c = 0
            for o in self.ops[e]:
                if o.dma:
                    continue
                if o.need:
                    c += 1
                o.val = c
        block = es.enter_context(nc.Block())
        for e in ENGINES:
            ops = self.ops[e]
            if not ops:
                continue

            def body(eng, ops=ops, e=e):
                hw = {}
                def w(sem, v):
                    if v > hw.get(sem.num, 0):
                        eng.wait_ge(sem, v)
                        hw[sem.num] = v
                for o in ops:
                    for dp in o.deps:
                        if dp.dma:
                            w(lanes[dp.lane], dp.lval)
                        elif (dp.engine == e
                                and e not in ("gpsimd", "vector")
                                and not paranoid and e not in par_eng):
                            continue
                        else:
                            w(sems[dp.engine], dp.val)
                    if o.dma and o.lval > 16:
                        w(lanes[o.lane], o.lval - 16)
                    inst = o.fn(eng) if o.fn is not None else None
                    if o.dma:
                        inst.then_inc(lanes[o.lane], 16)
                    elif o.need:
                        inst.then_inc(sems[e], 1)

            getattr(block, e)(body)


def build_program(taps=None, ntt=NTT, paranoid=False, stop=None, par_eng=()):
    taps = taps or {}
    nc = bass.Bass("TRN2", target_bir_lowering=False, debug=False,
                   detect_race_conditions=paranoid)
    x_d = nc.dram_tensor("x", [B_PER_CORE, T, C], F32, kind="ExternalInput").ap()
    w_d = {}
    for name in ("wq", "wk", "wv", "wo", "wf"):
        w_d[name] = nc.dram_tensor(name, [C, C], F32, kind="ExternalInput").ap()
    disha_d = nc.dram_tensor("disha", [64, C], F32, kind="ExternalInput").ap()
    bias_d = nc.dram_tensor("bias", [1, 4 * C], F32, kind="ExternalInput").ap()
    biasT_d = nc.dram_tensor("biasT", [128, 32], F32, kind="ExternalInput").ap()
    out_d = nc.dram_tensor("out", [B_PER_CORE, T, C], F32, kind="ExternalOutput").ap()
    tap_d = {}
    for tname, spec in {
        "q_fm": [128, 8, 256], "k_fm": [128, 8, 384], "v_tm": [128, 3, 1024],
        "attn": [128, 8, 256], "y": [128, 2, 1024], "pool": [64, 256],
        "z1": [128, 2, 1024],
    }.items():
        if tname in taps:
            dt = BF16 if tname in ("q_fm", "k_fm", "v_tm", "attn", "pool") else F32
            tap_d[tname] = nc.dram_tensor("tap_" + tname, spec, dt,
                                          kind="ExternalOutput").ap()

    S = Sched()
    es = ExitStack()
    with es:
        E = es.enter_context

        def sb(name, shape, dt):
            return E(nc.sbuf_tensor(name, shape, dt))

        wq_b = sb("wq_b", [128, 8, 1024], BF16)
        wk_b = sb("wk_b", [128, 8, 1024], BF16)
        wv_b = sb("wv_b", [128, 8, 1024], BF16)
        wo_b = sb("wo_b", [128, 8, 1024], BF16)
        wfT_b = sb("wfT_b", [128, 8, 1024], BF16)
        disha_b = sb("disha_b", [64, 1024], BF16)
        brow_bf = sb("brow_bf", [1, 4096], BF16)
        ones_t = sb("ones_t", [1, 256], BF16)
        eps_t = sb("eps_t", [128, 1], F32)
        biasT = sb("biasT_sb", [128, 40], F32)
        mask4 = sb("mask4", [128, 1024], BF16)
        maskf4 = sb("maskf4", [128, 1024], BF16)
        identb = sb("identb", [128, 128], BF16)
        gmat_b = sb("gmat_b", [128, 64], BF16)
        stage = sb("stage", [128, 1024], F32)
        identf = stage[:, 0:128]
        gtmp = stage[:, 128:256]
        maskstage = stage[:, 256:512]
        stage2 = sb("stage2", [128, 1024], F32)
        x_tm = [sb("x_tm0", [128, 2, 1024], F32), sb("x_tm1", [128, 2, 1024], F32)]
        x_bf = sb("x_bf", [128, 2048], BF16)
        x_fm = sb("x_fm", [128, 8, 256], BF16)
        q_fm = sb("q_fm", [128, 8, 256], BF16)
        k_fm = sb("k_fm", [128, 8, 384], BF16)
        v_tm = sb("v_tm", [128, 3, 1024], BF16)
        NWP = 3
        NWPB = 3
        p32 = [sb("p32_%d" % i, [128, 1024], F32) for i in range(NWP)]
        pbf = [sb("pbf_%d" % i, [128, 1024], BF16) for i in range(NWPB)]
        ptb = [sb("ptb_%d" % i, [128, 1024], BF16) for i in range(NWPB)]
        den = [sb("den_%d" % i, [128, 4], F32) for i in range(NWP)]
        rec = [sb("rec_%d" % i, [128, 4], F32) for i in range(NWP)]
        attn_fm = sb("attn_fm", [128, 8, 256], BF16)
        z1 = sb("z1", [128, 2, 1024], F32)
        y_t = sb("y_t", [128, 2, 1024], F32)
        y_bf = sb("y_bf", [128, 2048], BF16)
        y_fm = sb("y_fm", [128, 8, 256], BF16)
        zz = stage
        pool_b = sb("pool_b", [64, 256], BF16)
        outb = sb("outb", [128, 2, 1024], F32)
        sums = sb("sums", [128, 2], F32)
        sumsq = sb("sumsq", [128, 2], F32)
        mean_t = sb("mean_t", [128, 2], F32)
        msq = sb("msq", [128, 2], F32)
        var_t = sb("var_t", [128, 2], F32)
        lnv = sb("lnv", [128, 2], F32)
        rs_t = sb("rs_t", [128, 2], F32)
        sb_t = sb("sb_t", [128, 2], F32)
        ps_sc = [E(nc.psum_tensor("ps_sc0", [128, 1024], F32)),
                 E(nc.psum_tensor("ps_sc1", [128, 1024], F32))]
        ps_tr = [E(nc.psum_tensor("ps_tr0", [128, 1024], BF16)),
                 E(nc.psum_tensor("ps_tr1", [128, 1024], BF16))]
        ps_bg = E(nc.psum_tensor("ps_bg", [128, 1024], F32))

        Bw = {n: [Buf() for _ in range(NKC)] for n in ("wq", "wk", "wv", "wo")}
        Bw["wf"] = [Buf()] * NKC
        Bdisha, Bones, Beps, BbiasT = Buf(), Buf(), Buf(), Buf()
        Bbrowbf = Buf()
        Bmask4, Bmaskf4, Bidentb = Buf(), Buf(), Buf()
        Bgmat, Bstage = Buf(), Buf()
        Bstage2 = Buf()
        Bidentf = Bgtmp = Bstage
        Bxtm = [[Buf(), Buf()], [Buf(), Buf()]]
        Bxbf, Bxfm = Buf(), Buf()
        Bqfm, Bkfm, Bvtm = Buf(), Buf(), Buf()
        Bp32 = [Buf() for _ in range(NWP)]
        Bpbf = [Buf() for _ in range(NWPB)]
        Bptb = [Buf() for _ in range(NWPB)]
        Bden = [Buf() for _ in range(NWP)]
        Brec = [Buf() for _ in range(NWP)]
        Battn = [Buf(), Buf()]
        Bz1 = [Buf(), Buf()]
        By = [Buf(), Buf()]
        Bybf = [Buf(), Buf()]
        Byfm = Buf()
        Bzz, Bpool = Bstage, Buf()
        Boutb = [Buf(), Buf()]
        Bsums, Bsumsq, Bmean, Bmsq = Buf(), Buf(), Buf(), Buf()
        Bvar, Blnv, Brs, Bsb = Buf(), Buf(), Buf(), Buf()
        Bpsc = [Buf(), Buf()]
        Bpstr = [Buf(), Buf()]
        Bbg = [Buf(), Buf()]
        psf = [ps_bg, ps_sc[0], ps_sc[1]]
        Bps = [[Buf(), Buf()] for _ in range(3)]
        _rot = [0]

        def grab():
            i = _rot[0] % 3
            _rot[0] += 1
            return psf[i], Bps[i]

        xdma = {}

        def load_x(g):
            b, tt = g // 8, g % 8
            t0 = tt * TT
            par = g % 2
            ops = []
            for jt in range(2):
                ops.append(S.do(
                    "sync",
                    lambda e, b=b, t0=t0, jt=jt, par=par: e.dma_start(
                        out=x_tm[par][:, jt, :],
                        in_=x_d[b, t0 + 128 * jt:t0 + 128 * (jt + 1), :]),
                    writes=[Bxtm[par][jt]], dma=True))
            xdma[g] = ops

        load_x(0)

        S.do("vector", lambda e: e.memset(ones_t[:], 1.0), writes=[Bones])
        S.do("vector", lambda e: e.memset(eps_t[:], EPS), writes=[Beps])
        S.do("sync", lambda e: e.dma_start(out=biasT[:, 0:32], in_=biasT_d[:]),
             writes=[BbiasT], dma=True)
        S.do("vector", lambda e: e.tensor_scalar_mul(
            biasT[:, 32:40], biasT[:, 8:16], 0.125), writes=[BbiasT])
        for kq in range(4):
            S.do("sync", lambda e, kq=kq: e.dma_start(
                out=stage[0:1, :], in_=bias_d[0:1, 1024 * kq:1024 * (kq + 1)]),
                writes=[Bstage], dma=True)
            S.do("vector", lambda e, kq=kq: e.tensor_copy(
                brow_bf[0:1, 1024 * kq:1024 * (kq + 1)], stage[0:1, :]),
                reads=[Bstage], writes=[Bbrowbf])

        S.do("gpsimd", lambda e: e.memset(identf[:, :], 1.0), writes=[Bidentf])
        S.do("gpsimd", lambda e: e.affine_select(
            identf[:, :], identf[:, :], [[1, 128]], ALU.is_equal, 0.0,
            base=0, channel_multiplier=-1), writes=[Bidentf])
        S.do("vector", lambda e: e.tensor_copy(identb[:], identf[:, :]),
             reads=[Bidentf], writes=[Bidentb])
        S.do("gpsimd", lambda e: e.memset(gtmp[:], 1.0), writes=[Bgtmp])
        S.do("gpsimd", lambda e: e.affine_select(
            gtmp[:, 0:64], gtmp[:, 0:64], [[1, 64]], ALU.is_equal, 0.0,
            base=0, channel_multiplier=-1), writes=[Bgtmp])
        S.do("gpsimd", lambda e: e.affine_select(
            gtmp[:, 64:128], gtmp[:, 64:128], [[1, 64]], ALU.is_equal, 0.0,
            base=64, channel_multiplier=-1), writes=[Bgtmp])
        S.do("vector", lambda e: e.tensor_add(gtmp[:, 0:64], gtmp[:, 0:64],
                                              gtmp[:, 64:128]), writes=[Bgtmp])
        S.do("vector", lambda e: e.tensor_copy(gmat_b[:], gtmp[:, 0:64]),
             reads=[Bgtmp], writes=[Bgmat])
        S.do("gpsimd", lambda e: e.memset(maskstage[:, :], 0.0), writes=[Bstage])
        S.do("gpsimd", lambda e: e.affine_select(
            maskstage[:, :], maskstage[:, :], [[1, 256]], ALU.is_ge, MASKVAL,
            base=0, channel_multiplier=-1), writes=[Bstage])
        S.do("gpsimd", lambda e: e.affine_select(
            maskstage[:, :], maskstage[:, :], [[-1, 256]], ALU.is_ge, MASKVAL,
            base=128, channel_multiplier=1), writes=[Bstage])
        for i in range(4):
            S.do("vector", lambda e, i=i: e.tensor_copy(
                mask4[:, 256 * i:256 * (i + 1)], maskstage[:, :]),
                reads=[Bstage], writes=[Bmask4])
        S.do("vector", lambda e: e.memset(maskf4[:], MASKVAL), writes=[Bmaskf4])
        for i in range(4):
            S.do("vector", lambda e, i=i: e.tensor_copy(
                maskf4[:, 256 * i + 128:256 * i + 256], mask4[:, 128:256]),
                reads=[Bmask4], writes=[Bmaskf4])

        S.do("sync", lambda e: e.dma_start(out=stage[0:64, :], in_=disha_d[:]),
             writes=[Bstage], dma=True)
        S.do("vector", lambda e: e.tensor_copy(disha_b[:], stage[0:64, :]),
             reads=[Bstage], writes=[Bdisha])

        stg = [(stage, Bstage), (stage2, Bstage2)]
        sidx = 0
        for name, wb in (("wq", wq_b), ("wk", wk_b), ("wv", wv_b), ("wo", wo_b)):
            for kc in range(NKC):
                st_t, st_b = stg[sidx % 2]
                sidx += 1
                S.do("sync", lambda e, name=name, kc=kc, st_t=st_t: e.dma_start(
                    out=st_t[:], in_=w_d[name][128 * kc:128 * (kc + 1), :]),
                    writes=[st_b], dma=True)
                S.do("vector", lambda e, wb=wb, kc=kc, st_t=st_t: e.tensor_copy(
                    wb[:, kc, :], st_t[:]), reads=[st_b], writes=[Bw[name][kc]])
        out_dmas = []
        wave_ctr = [0]

        class _StopBuild(Exception):
            pass


        def emit_xcast(g):
            par = g % 2
            S.do("vector", lambda e, par=par: e.tensor_copy(
                x_bf[:], x_tm[par][:]),
                reads=[Bxtm[par][0], Bxtm[par][1]], writes=[Bxbf])

        def emit_xtr(g, jt):
            for kc in range(NKC):
                S.do("tensor", lambda e, jt=jt, kc=kc: e.transpose(
                    ps_tr[jt][:, 128 * kc:128 * (kc + 1)],
                    x_bf[:, 1024 * jt + 128 * kc:1024 * jt + 128 * (kc + 1)],
                    identb[:]), reads=[Bxbf, Bidentb], writes=[Bpstr[jt]])
            S.do("vector", lambda e, jt=jt: e.tensor_copy(
                x_fm[:, :, 128 * jt:128 * (jt + 1)], ps_tr[jt][:]),
                reads=[Bpstr[jt]], writes=[Bxfm])

        def emit_xpipe(g):
            b, tt = g // 8, g % 8
            par = g % 2
            if g + 1 < ntt:
                load_x(g + 1)
            if tt == 0:
                S.do("vector", lambda e: e.memset(k_fm[:, :, 0:128], 0.0),
                     writes=[Bkfm])
                S.do("vector", lambda e: e.memset(v_tm[:, 0, :], 0.0),
                     writes=[Bvtm])
            for name, wb, bi in (("wq", wq_b, 0), ("wk", wk_b, 1)):
                for half in range(2):
                    ps, Bp = grab()
                    for i in range(4):
                        ncx = 4 * half + i
                        reg = ps[:, 256 * i:256 * (i + 1)]
                        for kc in range(NKC):
                            S.do("tensor", lambda e, reg=reg, wb=wb, kc=kc,
                                 ncx=ncx: e.matmul(
                                     reg, wb[:, kc, 128 * ncx:128 * (ncx + 1)],
                                     x_fm[:, kc, :], start=(kc == 0),
                                     stop=(kc == NKC - 1)),
                                 reads=[Bw[name][kc], Bxfm], writes=[Bp[i // 2]])
                    for i in range(4):
                        ncx = 4 * half + i
                        if name == "wq":
                            S.do("scalar", lambda e, i=i, ncx=ncx, ps=ps:
                                 e.activation(
                                     q_fm[:, ncx, :],
                                     ps[:, 256 * i:256 * (i + 1)],
                                     AF.Identity,
                                     bias=biasT[:, ncx:ncx + 1], scale=1.0),
                                 reads=[Bp[i // 2], BbiasT], writes=[Bqfm])
                        else:
                            S.do("scalar", lambda e, i=i, ncx=ncx, ps=ps:
                                 e.activation(
                                     k_fm[:, ncx, 128:384],
                                     ps[:, 256 * i:256 * (i + 1)],
                                     AF.Identity,
                                     bias=biasT[:, 32 + ncx:33 + ncx],
                                     scale=0.125),
                                 reads=[Bp[i // 2], BbiasT], writes=[Bkfm])
            for jt in range(2):
                ps, Bp = grab()
                for nh in range(2):
                    reg = ps[:, 512 * nh:512 * (nh + 1)]
                    for kc in range(NKC):
                        S.do("tensor", lambda e, reg=reg, kc=kc, jt=jt, nh=nh:
                             e.matmul(reg,
                                      x_fm[:, kc, 128 * jt:128 * (jt + 1)],
                                      wv_b[:, kc, 512 * nh:512 * (nh + 1)],
                                      start=(kc == 0), stop=False),
                             reads=[Bw["wv"][kc], Bxfm], writes=[Bp[nh]])
                    S.do("tensor", lambda e, reg=reg, nh=nh: e.matmul(
                        reg, ones_t[:, 0:128],
                        brow_bf[0:1, 2048 + 512 * nh:2048 + 512 * (nh + 1)],
                        start=False, stop=True),
                        reads=[Bones, Bbrowbf], writes=[Bp[nh]])
                S.do("scalar", lambda e, jt=jt, ps=ps: e.activation(
                    v_tm[:, 1 + jt, :], ps[:], AF.Identity, bias=0.0, scale=1.0),
                    reads=[Bp[0], Bp[1]], writes=[Bvtm])
            if taps.get("q_fm") == g:
                S.do("sync", lambda e: e.dma_start(out=tap_d["q_fm"], in_=q_fm[:]),
                     reads=[Bqfm], dma=True)
                S.do("sync", lambda e: e.dma_start(out=tap_d["k_fm"], in_=k_fm[:]),
                     reads=[Bkfm], dma=True)
                S.do("sync", lambda e: e.dma_start(out=tap_d["v_tm"], in_=v_tm[:]),
                     reads=[Bvtm], dma=True)

        def _att_s1(g, w):
            first = (g % 8 == 0)
            qt, ww = w // 4, w % 4
            qs = 128 * qt
            sp = 1 + w % 2
            pairs = (2 * ww, 2 * ww + 1)
            mk = maskf4 if (first and qt == 0) else mask4
            mb = Bmaskf4 if (first and qt == 0) else Bmask4
            for pi, p2 in enumerate(pairs):
                for hp in range(2):
                    col = 512 * hp + 256 * pi
                    S.do("tensor", lambda e, sp=sp, col=col, hp=hp,
                         p2=p2, qs=qs: e.matmul(
                             psf[sp][:, col:col + 256],
                             q_fm[64 * hp:64 * (hp + 1), p2, qs:qs + 128],
                             k_fm[64 * hp:64 * (hp + 1), p2, qs:qs + 256],
                             start=True, stop=True,
                             tile_position=(64 * hp, 0)),
                         reads=[Bqfm, Bkfm], writes=[Bps[sp][hp]])
                for hp in range(2):
                    j = 2 * hp + pi
                    S.do("tensor", lambda e, sp=sp, j=j, mk=mk: e.matmul(
                        psf[sp][:, 256 * j:256 * (j + 1)], identb[:],
                        mk[:, 256 * j:256 * (j + 1)], start=False, stop=True,
                        skip_group_check=True),
                        reads=[mb, Bidentb], writes=[Bps[sp][j // 2]])

        def _att_s2(g, w):
            wp = w % NWP
            wb_ = w % NWPB
            sp = 1 + w % 2
            for j in range(4):
                S.do("scalar", lambda e, wp=wp, sp=sp, j=j: e.activation(
                    p32[wp][:, 256 * j:256 * (j + 1)],
                    psf[sp][:, 256 * j:256 * (j + 1)], AF.Exp,
                    bias=0.0, scale=1.0,
                    accum_out=den[wp][:, j:j + 1]),
                    reads=[Bps[sp][j // 2]],
                    writes=[Bp32[wp], Bden[wp]])
            S.do("vector", lambda e, wp=wp: e.reciprocal(
                rec[wp][:], den[wp][:]), reads=[Bden[wp]], writes=[Brec[wp]])
            for j in range(4):
                S.do("vector", lambda e, wp=wp, wb_=wb_, j=j: e.tensor_scalar_mul(
                    pbf[wb_][:, 256 * j:256 * (j + 1)],
                    p32[wp][:, 256 * j:256 * (j + 1)],
                    rec[wp][:, j:j + 1]),
                    reads=[Bp32[wp], Brec[wp]], writes=[Bpbf[wb_]])

        def _att_s3(g, w):
            wb_ = w % NWPB
            tp = w % 2
            for i in range(8):
                S.do("tensor", lambda e, wb_=wb_, tp=tp, i=i: e.transpose(
                    ps_tr[tp][:, 128 * i:128 * (i + 1)],
                    pbf[wb_][:, 128 * i:128 * (i + 1)], identb[:]),
                    reads=[Bpbf[wb_], Bidentb], writes=[Bpstr[tp]])
            S.do("vector", lambda e, wb_=wb_, tp=tp: e.tensor_copy(
                ptb[wb_][:], ps_tr[tp][:]), reads=[Bpstr[tp]],
                writes=[Bptb[wb_]])

        def _att_s4(g, w):
            wp = w % NWPB
            qt, ww = w // 4, w % 4
            qs = 128 * qt
            pairs = (2 * ww, 2 * ww + 1)
            for pi, p2 in enumerate(pairs):
                half = p2 // 4
                colq = 512 * half + 128 * (p2 % 4)
                for hp in range(2):
                    h = 2 * p2 + hp
                    j = 2 * hp + pi
                    for kb in range(2):
                        S.do("tensor", lambda e, wp=wp, hp=hp, h=h, j=j,
                             kb=kb, qt=qt, colq=colq: e.matmul(
                                 psf[0][64 * hp:64 * (hp + 1),
                                        colq:colq + 128],
                                 v_tm[:, qt + kb, 64 * h:64 * (h + 1)],
                                 ptb[wp][:, 256 * j + 128 * kb:
                                          256 * j + 128 * (kb + 1)],
                                 start=(kb == 0), stop=(kb == 1),
                                 tile_position=(0, 64 * hp)),
                             reads=[Bvtm, Bptb[wp]], writes=[Bps[0][half]])
            if ww in (1, 3):
                half = 0 if ww == 1 else 1
                S.do("scalar", lambda e, half=half, qs=qs, qt=qt: e.activation(
                    attn_fm[:, 4 * half:4 * (half + 1), qs:qs + 128],
                    psf[0][:, 512 * half:512 * (half + 1)],
                    AF.Identity, bias=0.0, scale=1.0),
                    reads=[Bps[0][half]], writes=[Battn[qt]])

        STAGGER = 1

        def emit_attention_all(g):
            for s in range(12):
                if s < 8:
                    _att_s1(g, s)
                if 1 <= s <= 8:
                    _att_s2(g, s - 1)
                if s in (1, 2) and g + 1 < ntt:
                    emit_xtr(g + 1, s - 1)
                if 3 <= s <= 10:
                    _att_s3(g, s - 3)
                if 4 <= s <= 11:
                    _att_s4(g, s - 4)

        def emit_wo_mm(g, jt):
            ps, Bp = grab()
            for nh in range(2):
                reg = ps[:, 512 * nh:512 * (nh + 1)]
                for kc in range(NKC):
                    S.do("tensor", lambda e, reg=reg, kc=kc, jt=jt, nh=nh:
                         e.matmul(reg,
                                  attn_fm[:, kc, 128 * jt:128 * (jt + 1)],
                                  wo_b[:, kc, 512 * nh:512 * (nh + 1)],
                                  start=(kc == 0), stop=False),
                         reads=[Battn[jt], Bw["wo"][kc]], writes=[Bp[nh]])
                S.do("tensor", lambda e, reg=reg, nh=nh: e.matmul(
                    reg, ones_t[:, 0:128],
                    brow_bf[0:1, 3072 + 512 * nh:3072 + 512 * (nh + 1)],
                    start=False, stop=True),
                    reads=[Bones, Bbrowbf], writes=[Bp[nh]])
            return ps, Bp

        def emit_ln1_y(g, jt, ps, Bp):
            par = g % 2
            _ln(S, nc, "vector", jt, par, ps, Bp, x_tm, Bxtm, z1, Bz1,
                y_t, By, zz, Bzz, sums, Bsums, sumsq, Bsumsq, mean_t, Bmean,
                msq, Bmsq, var_t, Bvar, lnv, Blnv, rs_t, Brs, sb_t, Bsb,
                eps_t, Beps, resid_src="x")
            S.do("vector", lambda e, jt=jt: e.tensor_copy(
                y_bf[:, 1024 * jt:1024 * (jt + 1)], y_t[:, jt, :]),
                reads=[By[jt]], writes=[Bybf[jt]])

        def emit_ytr(g, jt):
            for kc in range(NKC):
                S.do("tensor", lambda e, jt=jt, kc=kc: e.transpose(
                    ps_tr[jt][:, 128 * kc:128 * (kc + 1)],
                    y_bf[:, 1024 * jt + 128 * kc:1024 * jt + 128 * (kc + 1)],
                    identb[:]), reads=[Bybf[jt], Bidentb], writes=[Bpstr[jt]])
            S.do("vector", lambda e, jt=jt: e.tensor_copy(
                y_fm[:, :, 128 * jt:128 * (jt + 1)], ps_tr[jt][:]),
                reads=[Bpstr[jt]], writes=[Byfm])

        def emit_ffn_out(g, jt):
            b, tt = g // 8, g % 8
            t0 = tt * TT
            par = g % 2
            ps, Bp = grab()
            for nh in range(2):
                reg = ps[:, 512 * nh:512 * (nh + 1)]
                for kc in range(NKC):
                    S.do("tensor", lambda e, reg=reg, kc=kc, jt=jt, nh=nh:
                         e.matmul(reg,
                                  y_fm[:, kc, 128 * jt:128 * (jt + 1)],
                                  wfT_b[:, kc, 512 * nh:512 * (nh + 1)],
                                  start=(kc == 0), stop=False),
                         reads=[Byfm, Bw["wf"][kc]], writes=[Bp[nh]])
                S.do("tensor", lambda e, reg=reg, jt=jt, nh=nh: e.matmul(
                    reg, pool_b[0:64, 128 * jt:128 * (jt + 1)],
                    disha_b[0:64, 512 * nh:512 * (nh + 1)],
                    start=False, stop=True),
                    reads=[Bpool, Bdisha], writes=[Bp[nh]])
            _ln(S, nc, "vector", jt, par, ps, Bp, None, None, z1, Bz1,
                outb, Boutb, zz, Bzz, sums, Bsums, sumsq, Bsumsq,
                mean_t, Bmean, msq, Bmsq, var_t, Bvar, lnv, Blnv, rs_t, Brs,
                sb_t, Bsb, eps_t, Beps, resid_src="y", y_t=y_t, By=By)
            op = S.do("sync", lambda e, b=b, t0=t0, jt=jt:
                      e.dma_start(
                          out=out_d[b, t0 + 128 * jt:t0 + 128 * (jt + 1), :],
                          in_=outb[:, jt, :]),
                      reads=[Boutb[jt]], dma=True)
            out_dmas.append(op)

        emit_xcast(0)
        emit_xtr(0, 0)
        emit_xtr(0, 1)
        emit_xpipe(0)
        for ncx in range(NKC):
            st_t, st_b = stg[sidx % 2]
            sidx += 1
            S.do("sync", lambda e, ncx=ncx, st_t=st_t: e.dma_start(
                out=st_t[:], in_=w_d["wf"][128 * ncx:128 * (ncx + 1), :]),
                writes=[st_b], dma=True)
            S.do("vector", lambda e, st_t=st_t: e.tensor_copy(
                pbf[0][:], st_t[:]),
                 reads=[st_b], writes=[Bpbf[0]])
            for kc in range(NKC):
                S.do("tensor", lambda e, kc=kc: e.transpose(
                    ps_tr[0][:, 128 * kc:128 * (kc + 1)],
                    pbf[0][:, 128 * kc:128 * (kc + 1)], identb[:]),
                    reads=[Bpbf[0], Bidentb], writes=[Bpstr[0]])
            S.do("scalar", lambda e, ncx=ncx: e.activation(
                wfT_b[:, :, 128 * ncx:128 * (ncx + 1)], ps_tr[0][:],
                AF.Identity, bias=0.0, scale=1.0),
                reads=[Bpstr[0]], writes=[Bw["wf"][0]])


        for g in range(ntt):
            tt = g % 8
            if g + 1 < ntt:
                emit_xcast(g + 1)
            emit_attention_all(g)
            ps0, Bp0 = emit_wo_mm(g, 0)
            ps1, Bp1 = emit_wo_mm(g, 1)
            emit_ln1_y(g, 0, ps0, Bp0)
            emit_ln1_y(g, 1, ps1, Bp1)
            if taps.get("attn") == g:
                S.do("sync", lambda e: e.dma_start(out=tap_d["attn"],
                                                   in_=attn_fm[:]),
                     reads=[Battn[0], Battn[1]], dma=True)
            if taps.get("y") == g:
                S.do("sync", lambda e: e.dma_start(out=tap_d["y"], in_=y_t[:]),
                     reads=[By[0], By[1]], dma=True)
            if taps.get("z1") == g:
                S.do("sync", lambda e: e.dma_start(out=tap_d["z1"], in_=z1[:]),
                     reads=[Bz1[0], Bz1[1]], dma=True)
            if tt < 7:
                S.do("vector", lambda e: e.tensor_copy(
                    k_fm[:, :, 0:128], k_fm[:, :, 256:384]), writes=[Bkfm])
                S.do("vector", lambda e: e.tensor_copy(
                    v_tm[:, 0, :], v_tm[:, 2, :]), writes=[Bvtm])
            if g + 1 < ntt:
                emit_xpipe(g + 1)
            emit_ytr(g, 0)
            emit_ytr(g, 1)
            ps, Bp = grab()
            for kc in range(NKC):
                S.do("tensor", lambda e, kc=kc, ps=ps: e.matmul(
                    ps[0:64, 0:256], gmat_b[:], y_fm[:, kc, :],
                    start=(kc == 0), stop=(kc == NKC - 1)),
                    reads=[Bgmat, Byfm], writes=[Bp[0]])
            S.do("scalar", lambda e, ps=ps: e.activation(
                pool_b[:], ps[0:64, 0:256], AF.Identity, bias=0.0, scale=1.0),
                reads=[Bp[0]], writes=[Bpool])
            if taps.get("pool") == g:
                S.do("sync", lambda e: e.dma_start(out=tap_d["pool"],
                                                   in_=pool_b[:]),
                     reads=[Bpool], dma=True)
            emit_ffn_out(g, 0)
            emit_ffn_out(g, 1)
        S.do("sync", None, deps=out_dmas)

        S.finalize(nc, es, paranoid, par_eng)
    return nc


def _ln(S, nc, veng, jt, par, ps_bg, Bbg, x_tm, Bxtm, z1, Bz1, dst, Bdst,
        zz, Bzz, sums, Bsums, sumsq, Bsumsq, mean_t, Bmean, msq, Bmsq,
        var_t, Bvar, lnv, Blnv, rs_t, Brs, sb_t, Bsb, eps_t, Beps,
        resid_src="x", y_t=None, By=None):
    if resid_src == "x":
        resid = x_tm[par][:, jt, :]
        Bres = Bxtm[par][jt]
    else:
        resid = y_t[:, jt, :]
        Bres = By[jt]
    S.do("vector", lambda e: e.scalar_tensor_tensor(
        z1[:, jt, :], ps_bg[:], 1.0, resid, ALU.mult, ALU.add,
        accum_out=sums[:, jt:jt + 1]),
        reads=[Bbg[0], Bbg[1], Bres], writes=[Bz1[jt], Bsums])
    S.do("vector", lambda e: e.scalar_tensor_tensor(
        zz[:], z1[:, jt, :], 1.0, z1[:, jt, :], ALU.mult, ALU.mult,
        accum_out=sumsq[:, jt:jt + 1]),
        reads=[Bz1[jt]], writes=[Bzz, Bsumsq])
    S.do("vector", lambda e: e.tensor_scalar_mul(
        mean_t[:, jt:jt + 1], sums[:, jt:jt + 1], 1.0 / C),
        reads=[Bsums], writes=[Bmean])
    S.do("vector", lambda e: e.tensor_mul(
        msq[:, jt:jt + 1], mean_t[:, jt:jt + 1], mean_t[:, jt:jt + 1]),
        reads=[Bmean], writes=[Bmsq])
    S.do("vector", lambda e: e.scalar_tensor_tensor(
        var_t[:, jt:jt + 1], sumsq[:, jt:jt + 1], 1.0 / C, msq[:, jt:jt + 1],
        ALU.mult, ALU.subtract),
        reads=[Bsumsq, Bmsq], writes=[Bvar])
    S.do("scalar", lambda e: e.activation(
        lnv[:, jt:jt + 1], var_t[:, jt:jt + 1], AF.Ln, bias=eps_t[:], scale=1.0),
        reads=[Bvar, Beps], writes=[Blnv])
    S.do("scalar", lambda e: e.activation(
        rs_t[:, jt:jt + 1], lnv[:, jt:jt + 1], AF.Exp, bias=0.0, scale=-0.5),
        reads=[Blnv], writes=[Brs])
    S.do("vector", lambda e: e.tensor_scalar(
        sb_t[:, jt:jt + 1], mean_t[:, jt:jt + 1], rs_t[:, jt:jt + 1], -1.0,
        ALU.mult, ALU.mult),
        reads=[Bmean, Brs], writes=[Bsb])
    S.do("scalar", lambda e: e.activation(
        dst[:, jt, :], z1[:, jt, :], AF.Identity,
        bias=sb_t[:, jt:jt + 1], scale=rs_t[:, jt:jt + 1]),
        reads=[Bz1[jt], Brs, Bsb], writes=[Bdst[jt]])


_NC_CACHE = {}


def _get_nc(taps_key=None):
    if taps_key not in _NC_CACHE:
        _NC_CACHE[taps_key] = build_program(dict(taps_key) if taps_key else None)
    return _NC_CACHE[taps_key]


def kernel(L0_attn_x, L0_attn_Wq, L0_attn_bq, L0_attn_Wk, L0_attn_bk,
           L0_attn_Wv, L0_attn_bv, L0_attn_Wo, L0_attn_bo,
           L0_ffn_weight, L0_ffn_disha):
    x = np.ascontiguousarray(np.asarray(L0_attn_x))
    biases = np.ascontiguousarray(np.stack([
        np.asarray(L0_attn_bq), np.asarray(L0_attn_bk),
        np.asarray(L0_attn_bv), np.asarray(L0_attn_bo)]).reshape(1, 4 * 1024))
    biasT = np.ascontiguousarray(np.stack(
        [np.asarray(b).reshape(8, 128).T for b in
         (L0_attn_bq, L0_attn_bk, L0_attn_bv, L0_attn_bo)],
        axis=1).reshape(128, 32))
    common = {
        "biasT": biasT,
        "wq": np.ascontiguousarray(np.asarray(L0_attn_Wq)),
        "wk": np.ascontiguousarray(np.asarray(L0_attn_Wk)),
        "wv": np.ascontiguousarray(np.asarray(L0_attn_Wv)),
        "wo": np.ascontiguousarray(np.asarray(L0_attn_Wo)),
        "wf": np.ascontiguousarray(np.asarray(L0_ffn_weight)),
        "disha": np.ascontiguousarray(np.asarray(L0_ffn_disha)),
        "bias": biases,
    }
    in_maps = [dict(common, x=np.ascontiguousarray(x[2 * c:2 * c + 2]))
               for c in range(8)]
    nc = _get_nc()
    res = run_bass_kernel_spmd(nc, in_maps, core_ids=list(range(8)))
    out = np.concatenate([res.results[c]["out"] for c in range(8)], axis=0)
    return out.astype(np.float32, copy=False)


# version 27
# speedup vs baseline: 1.2996x; 1.0417x over previous
import numpy as np
import concourse.bass as bass
import concourse.mybir as mybir
from concourse.bass_utils import run_bass_kernel_spmd
from contextlib import ExitStack

F32 = mybir.dt.float32
BF16 = mybir.dt.bfloat16
AF = mybir.ActivationFunctionType
ALU = mybir.AluOpType

ENGINES = ("sync", "tensor", "vector", "scalar", "gpsimd")

B_PER_CORE = 2
T = 2048
C = 1024
NKC = 8
TT = 256
NTT = (B_PER_CORE * T) // TT
MASKVAL = -30000.0
EPS = 1e-5


class _Op:
    __slots__ = ("engine", "fn", "deps", "need", "dma", "val", "lane", "lval")

    def __init__(self, engine, fn, deps, dma):
        self.engine, self.fn, self.deps, self.dma = engine, fn, deps, dma
        self.need = False
        self.val = 0
        self.lane = -1
        self.lval = 0


class Buf:
    __slots__ = ("w", "rs")

    def __init__(self):
        self.w = None
        self.rs = []


class Sched:
    def __init__(self):
        self.ops = {e: [] for e in ENGINES}

    def do(self, engine, fn, reads=(), writes=(), deps=(), dma=False):
        d = [x for x in deps if x is not None]
        for b in reads:
            if b.w is not None:
                d.append(b.w)
        for b in writes:
            if b.w is not None:
                d.append(b.w)
            d.extend(b.rs)
        op = _Op(engine, fn, d, dma)
        self.ops[engine].append(op)
        for b in reads:
            b.rs.append(op)
        for b in writes:
            b.w = op
            b.rs = []
        return op

    NLANES = 12

    def finalize(self, nc, es, paranoid=False, par_eng=()):
        for e in ENGINES:
            lane_uses = [0] * self.NLANES
            nxt = 0
            for o in self.ops[e]:
                if o.dma:
                    o.need = True
                    o.lane = nxt
                    lane_uses[nxt] += 1
                    o.lval = 16 * lane_uses[nxt]
                    nxt = (nxt + 1) % self.NLANES
        for e in ENGINES:
            for o in self.ops[e]:
                for dp in o.deps:
                    if (dp.engine != o.engine
                            or dp.engine in ("gpsimd", "vector")
                            or paranoid or dp.engine in par_eng):
                        dp.need = True
        sems = {e: es.enter_context(nc.semaphore("ch_" + e)) for e in ENGINES}
        lanes = [es.enter_context(nc.semaphore("lane%d" % i))
                 for i in range(self.NLANES)]
        for e in ENGINES:
            c = 0
            for o in self.ops[e]:
                if o.dma:
                    continue
                if o.need:
                    c += 1
                o.val = c
        block = es.enter_context(nc.Block())
        for e in ENGINES:
            ops = self.ops[e]
            if not ops:
                continue

            def body(eng, ops=ops, e=e):
                hw = {}
                def w(sem, v):
                    if v > hw.get(sem.num, 0):
                        eng.wait_ge(sem, v)
                        hw[sem.num] = v
                for o in ops:
                    for dp in o.deps:
                        if dp.dma:
                            w(lanes[dp.lane], dp.lval)
                        elif (dp.engine == e
                                and e not in ("gpsimd", "vector")
                                and not paranoid and e not in par_eng):
                            continue
                        else:
                            w(sems[dp.engine], dp.val)
                    if o.dma and o.lval > 16:
                        w(lanes[o.lane], o.lval - 16)
                    inst = o.fn(eng) if o.fn is not None else None
                    if o.dma:
                        inst.then_inc(lanes[o.lane], 16)
                    elif o.need:
                        inst.then_inc(sems[e], 1)

            getattr(block, e)(body)


def build_program(taps=None, ntt=NTT, paranoid=False, stop=None, par_eng=()):
    taps = taps or {}
    nc = bass.Bass("TRN2", target_bir_lowering=False, debug=False,
                   detect_race_conditions=paranoid)
    x_d = nc.dram_tensor("x", [B_PER_CORE, T, C], F32, kind="ExternalInput").ap()
    w_d = {}
    for name in ("wq", "wk", "wv", "wo", "wf"):
        w_d[name] = nc.dram_tensor(name, [C, C], F32, kind="ExternalInput").ap()
    disha_d = nc.dram_tensor("disha", [64, C], F32, kind="ExternalInput").ap()
    bias_d = nc.dram_tensor("bias", [1, 4 * C], F32, kind="ExternalInput").ap()
    biasT_d = nc.dram_tensor("biasT", [128, 32], F32, kind="ExternalInput").ap()
    out_d = nc.dram_tensor("out", [B_PER_CORE, T, C], F32, kind="ExternalOutput").ap()
    tap_d = {}
    for tname, spec in {
        "q_fm": [128, 8, 256], "k_fm": [128, 8, 384], "v_tm": [128, 3, 1024],
        "attn": [128, 8, 256], "y": [128, 2, 1024], "pool": [64, 256],
        "z1": [128, 2, 1024],
    }.items():
        if tname in taps:
            dt = BF16 if tname in ("q_fm", "k_fm", "v_tm", "attn", "pool") else F32
            tap_d[tname] = nc.dram_tensor("tap_" + tname, spec, dt,
                                          kind="ExternalOutput").ap()

    S = Sched()
    es = ExitStack()
    with es:
        E = es.enter_context

        def sb(name, shape, dt):
            return E(nc.sbuf_tensor(name, shape, dt))

        wq_b = sb("wq_b", [128, 8, 1024], BF16)
        wk_b = sb("wk_b", [128, 8, 1024], BF16)
        wv_b = sb("wv_b", [128, 8, 1024], BF16)
        wo_b = sb("wo_b", [128, 8, 1024], BF16)
        wfT_b = sb("wfT_b", [128, 8, 1024], BF16)
        disha_b = sb("disha_b", [64, 1024], BF16)
        brow_bf = sb("brow_bf", [1, 4096], BF16)
        ones_t = sb("ones_t", [1, 256], BF16)
        eps_t = sb("eps_t", [128, 1], F32)
        biasT = sb("biasT_sb", [128, 40], F32)
        mask4 = sb("mask4", [128, 1024], BF16)
        maskf4 = sb("maskf4", [128, 1024], BF16)
        identb = sb("identb", [128, 128], BF16)
        gmat_b = sb("gmat_b", [128, 64], BF16)
        stage = sb("stage", [128, 1024], F32)
        identf = stage[:, 0:128]
        gtmp = stage[:, 128:256]
        maskstage = stage[:, 256:512]
        stage2 = sb("stage2", [128, 1024], F32)
        x_tm = [sb("x_tm0", [128, 2, 1024], F32), sb("x_tm1", [128, 2, 1024], F32)]
        x_bf = sb("x_bf", [128, 2048], BF16)
        x_fm = sb("x_fm", [128, 8, 256], BF16)
        q_fm = sb("q_fm", [128, 8, 256], BF16)
        k_fm = sb("k_fm", [128, 8, 384], BF16)
        v_tm = sb("v_tm", [128, 3, 1024], BF16)
        NWP = 3
        NWPB = 3
        p32 = [sb("p32_%d" % i, [128, 1024], F32) for i in range(NWP)]
        pbf = [sb("pbf_%d" % i, [128, 1024], BF16) for i in range(NWPB)]
        ptb = [sb("ptb_%d" % i, [128, 1024], BF16) for i in range(NWPB)]
        den = [sb("den_%d" % i, [128, 4], F32) for i in range(NWP)]
        rec = [sb("rec_%d" % i, [128, 4], F32) for i in range(NWP)]
        attn_fm = sb("attn_fm", [128, 8, 256], BF16)
        z1 = sb("z1", [128, 2, 1024], F32)
        y_t = sb("y_t", [128, 2, 1024], F32)
        y_bf = sb("y_bf", [128, 2048], BF16)
        y_fm = sb("y_fm", [128, 8, 256], BF16)
        zz = stage
        pool_b = sb("pool_b", [64, 256], BF16)
        outb = sb("outb", [128, 2, 1024], F32)
        sums = sb("sums", [128, 2], F32)
        sumsq = sb("sumsq", [128, 2], F32)
        mean_t = sb("mean_t", [128, 2], F32)
        msq = sb("msq", [128, 2], F32)
        var_t = sb("var_t", [128, 2], F32)
        lnv = sb("lnv", [128, 2], F32)
        rs_t = sb("rs_t", [128, 2], F32)
        sb_t = sb("sb_t", [128, 2], F32)
        ps_sc = [E(nc.psum_tensor("ps_sc0", [128, 1024], F32)),
                 E(nc.psum_tensor("ps_sc1", [128, 1024], F32))]
        ps_tr = [E(nc.psum_tensor("ps_tr0", [128, 1024], BF16)),
                 E(nc.psum_tensor("ps_tr1", [128, 1024], BF16))]
        ps_bg = E(nc.psum_tensor("ps_bg", [128, 1024], F32))

        Bw = {n: [Buf() for _ in range(NKC)] for n in ("wq", "wk", "wv", "wo")}
        Bw["wf"] = [Buf()] * NKC
        Bdisha, Bones, Beps, BbiasT = Buf(), Buf(), Buf(), Buf()
        Bbrowbf = Buf()
        Bmask4, Bmaskf4, Bidentb = Buf(), Buf(), Buf()
        Bgmat, Bstage = Buf(), Buf()
        Bstage2 = Buf()
        Bidentf = Bgtmp = Bstage
        Bxtm = [[Buf(), Buf()], [Buf(), Buf()]]
        Bxbf, Bxfm = Buf(), Buf()
        Bqfm, Bkfm, Bvtm = Buf(), Buf(), Buf()
        Bp32 = [Buf() for _ in range(NWP)]
        Bpbf = [Buf() for _ in range(NWPB)]
        Bptb = [Buf() for _ in range(NWPB)]
        Bden = [Buf() for _ in range(NWP)]
        Brec = [Buf() for _ in range(NWP)]
        Battn = [Buf(), Buf()]
        Bz1 = [Buf(), Buf()]
        By = [Buf(), Buf()]
        Bybf = [Buf(), Buf()]
        Byfm = Buf()
        Bzz, Bpool = Bstage, Buf()
        Boutb = [Buf(), Buf()]
        Bsums, Bsumsq, Bmean, Bmsq = Buf(), Buf(), Buf(), Buf()
        Bvar, Blnv, Brs, Bsb = Buf(), Buf(), Buf(), Buf()
        Bpsc = [Buf(), Buf()]
        Bpstr = [Buf(), Buf()]
        Bbg = [Buf(), Buf()]
        psf = [ps_bg, ps_sc[0], ps_sc[1]]
        Bps = [[Buf(), Buf()] for _ in range(3)]
        _rot = [0]

        def grab():
            i = _rot[0] % 3
            _rot[0] += 1
            return psf[i], Bps[i]

        xdma = {}

        def load_x(g):
            b, tt = g // 8, g % 8
            t0 = tt * TT
            par = g % 2
            ops = []
            for jt in range(2):
                ops.append(S.do(
                    "sync",
                    lambda e, b=b, t0=t0, jt=jt, par=par: e.dma_start(
                        out=x_tm[par][:, jt, :],
                        in_=x_d[b, t0 + 128 * jt:t0 + 128 * (jt + 1), :]),
                    writes=[Bxtm[par][jt]], dma=True))
            xdma[g] = ops

        load_x(0)

        S.do("vector", lambda e: e.memset(ones_t[:], 1.0), writes=[Bones])
        S.do("vector", lambda e: e.memset(eps_t[:], EPS), writes=[Beps])
        S.do("sync", lambda e: e.dma_start(out=biasT[:, 0:32], in_=biasT_d[:]),
             writes=[BbiasT], dma=True)
        S.do("vector", lambda e: e.tensor_scalar_mul(
            biasT[:, 32:40], biasT[:, 8:16], 0.125), writes=[BbiasT])
        for kq in range(4):
            S.do("sync", lambda e, kq=kq: e.dma_start(
                out=stage[0:1, :], in_=bias_d[0:1, 1024 * kq:1024 * (kq + 1)]),
                writes=[Bstage], dma=True)
            S.do("vector", lambda e, kq=kq: e.tensor_copy(
                brow_bf[0:1, 1024 * kq:1024 * (kq + 1)], stage[0:1, :]),
                reads=[Bstage], writes=[Bbrowbf])

        S.do("gpsimd", lambda e: e.memset(identf[:, :], 1.0), writes=[Bidentf])
        S.do("gpsimd", lambda e: e.affine_select(
            identf[:, :], identf[:, :], [[1, 128]], ALU.is_equal, 0.0,
            base=0, channel_multiplier=-1), writes=[Bidentf])
        S.do("vector", lambda e: e.tensor_copy(identb[:], identf[:, :]),
             reads=[Bidentf], writes=[Bidentb])
        S.do("gpsimd", lambda e: e.memset(gtmp[:], 1.0), writes=[Bgtmp])
        S.do("gpsimd", lambda e: e.affine_select(
            gtmp[:, 0:64], gtmp[:, 0:64], [[1, 64]], ALU.is_equal, 0.0,
            base=0, channel_multiplier=-1), writes=[Bgtmp])
        S.do("gpsimd", lambda e: e.affine_select(
            gtmp[:, 64:128], gtmp[:, 64:128], [[1, 64]], ALU.is_equal, 0.0,
            base=64, channel_multiplier=-1), writes=[Bgtmp])
        S.do("vector", lambda e: e.tensor_add(gtmp[:, 0:64], gtmp[:, 0:64],
                                              gtmp[:, 64:128]), writes=[Bgtmp])
        S.do("vector", lambda e: e.tensor_copy(gmat_b[:], gtmp[:, 0:64]),
             reads=[Bgtmp], writes=[Bgmat])
        S.do("gpsimd", lambda e: e.memset(maskstage[:, :], 0.0), writes=[Bstage])
        S.do("gpsimd", lambda e: e.affine_select(
            maskstage[:, :], maskstage[:, :], [[1, 256]], ALU.is_ge, MASKVAL,
            base=0, channel_multiplier=-1), writes=[Bstage])
        S.do("gpsimd", lambda e: e.affine_select(
            maskstage[:, :], maskstage[:, :], [[-1, 256]], ALU.is_ge, MASKVAL,
            base=128, channel_multiplier=1), writes=[Bstage])
        for i in range(4):
            S.do("vector", lambda e, i=i: e.tensor_copy(
                mask4[:, 256 * i:256 * (i + 1)], maskstage[:, :]),
                reads=[Bstage], writes=[Bmask4])
        S.do("vector", lambda e: e.memset(maskf4[:], MASKVAL), writes=[Bmaskf4])
        for i in range(4):
            S.do("vector", lambda e, i=i: e.tensor_copy(
                maskf4[:, 256 * i + 128:256 * i + 256], mask4[:, 128:256]),
                reads=[Bmask4], writes=[Bmaskf4])

        S.do("sync", lambda e: e.dma_start(out=stage[0:64, :], in_=disha_d[:]),
             writes=[Bstage], dma=True)
        S.do("vector", lambda e: e.tensor_copy(disha_b[:], stage[0:64, :]),
             reads=[Bstage], writes=[Bdisha])

        stg = [(stage, Bstage), (stage2, Bstage2)]
        sidx = 0
        for name, wb in (("wq", wq_b), ("wk", wk_b), ("wv", wv_b), ("wo", wo_b)):
            for kc in range(NKC):
                st_t, st_b = stg[sidx % 2]
                sidx += 1
                S.do("sync", lambda e, name=name, kc=kc, st_t=st_t: e.dma_start(
                    out=st_t[:], in_=w_d[name][128 * kc:128 * (kc + 1), :]),
                    writes=[st_b], dma=True)
                S.do("vector", lambda e, wb=wb, kc=kc, st_t=st_t: e.tensor_copy(
                    wb[:, kc, :], st_t[:]), reads=[st_b], writes=[Bw[name][kc]])
        out_dmas = []
        wave_ctr = [0]

        class _StopBuild(Exception):
            pass


        def emit_xcast(g):
            par = g % 2
            S.do("vector", lambda e, par=par: e.tensor_copy(
                x_bf[:], x_tm[par][:]),
                reads=[Bxtm[par][0], Bxtm[par][1]], writes=[Bxbf])

        def emit_xtr(g, jt):
            for kc in range(NKC):
                S.do("tensor", lambda e, jt=jt, kc=kc: e.transpose(
                    ps_tr[jt][:, 128 * kc:128 * (kc + 1)],
                    x_bf[:, 1024 * jt + 128 * kc:1024 * jt + 128 * (kc + 1)],
                    identb[:]), reads=[Bxbf, Bidentb], writes=[Bpstr[jt]])
            S.do("vector", lambda e, jt=jt: e.tensor_copy(
                x_fm[:, :, 128 * jt:128 * (jt + 1)], ps_tr[jt][:]),
                reads=[Bpstr[jt]], writes=[Bxfm])

        def emit_xpipe(g):
            b, tt = g // 8, g % 8
            par = g % 2
            if g + 1 < ntt:
                load_x(g + 1)
            if tt == 0:
                S.do("vector", lambda e: e.memset(k_fm[:, :, 0:128], 0.0),
                     writes=[Bkfm])
                S.do("vector", lambda e: e.memset(v_tm[:, 0, :], 0.0),
                     writes=[Bvtm])
            for name, wb, bi in (("wq", wq_b, 0), ("wk", wk_b, 1)):
                for half in range(2):
                    ps, Bp = grab()
                    for i in range(4):
                        ncx = 4 * half + i
                        reg = ps[:, 256 * i:256 * (i + 1)]
                        for kc in range(NKC):
                            S.do("tensor", lambda e, reg=reg, wb=wb, kc=kc,
                                 ncx=ncx: e.matmul(
                                     reg, wb[:, kc, 128 * ncx:128 * (ncx + 1)],
                                     x_fm[:, kc, :], start=(kc == 0),
                                     stop=(kc == NKC - 1)),
                                 reads=[Bw[name][kc], Bxfm], writes=[Bp[i // 2]])
                    for i in range(4):
                        ncx = 4 * half + i
                        if name == "wq":
                            S.do("scalar", lambda e, i=i, ncx=ncx, ps=ps:
                                 e.activation(
                                     q_fm[:, ncx, :],
                                     ps[:, 256 * i:256 * (i + 1)],
                                     AF.Identity,
                                     bias=biasT[:, ncx:ncx + 1], scale=1.0),
                                 reads=[Bp[i // 2], BbiasT], writes=[Bqfm])
                        else:
                            S.do("scalar", lambda e, i=i, ncx=ncx, ps=ps:
                                 e.activation(
                                     k_fm[:, ncx, 128:384],
                                     ps[:, 256 * i:256 * (i + 1)],
                                     AF.Identity,
                                     bias=biasT[:, 32 + ncx:33 + ncx],
                                     scale=0.125),
                                 reads=[Bp[i // 2], BbiasT], writes=[Bkfm])
            for jt in range(2):
                ps, Bp = grab()
                for nh in range(2):
                    reg = ps[:, 512 * nh:512 * (nh + 1)]
                    for kc in range(NKC):
                        S.do("tensor", lambda e, reg=reg, kc=kc, jt=jt, nh=nh:
                             e.matmul(reg,
                                      x_fm[:, kc, 128 * jt:128 * (jt + 1)],
                                      wv_b[:, kc, 512 * nh:512 * (nh + 1)],
                                      start=(kc == 0), stop=False),
                             reads=[Bw["wv"][kc], Bxfm], writes=[Bp[nh]])
                    S.do("tensor", lambda e, reg=reg, nh=nh: e.matmul(
                        reg, ones_t[:, 0:128],
                        brow_bf[0:1, 2048 + 512 * nh:2048 + 512 * (nh + 1)],
                        start=False, stop=True),
                        reads=[Bones, Bbrowbf], writes=[Bp[nh]])
                S.do("scalar", lambda e, jt=jt, ps=ps: e.activation(
                    v_tm[:, 1 + jt, :], ps[:], AF.Identity, bias=0.0, scale=1.0),
                    reads=[Bp[0], Bp[1]], writes=[Bvtm])
            if taps.get("q_fm") == g:
                S.do("sync", lambda e: e.dma_start(out=tap_d["q_fm"], in_=q_fm[:]),
                     reads=[Bqfm], dma=True)
                S.do("sync", lambda e: e.dma_start(out=tap_d["k_fm"], in_=k_fm[:]),
                     reads=[Bkfm], dma=True)
                S.do("sync", lambda e: e.dma_start(out=tap_d["v_tm"], in_=v_tm[:]),
                     reads=[Bvtm], dma=True)

        def _att_s1(g, w):
            first = (g % 8 == 0)
            qt, ww = w // 4, w % 4
            qs = 128 * qt
            sp = 1 + w % 2
            pairs = (2 * ww, 2 * ww + 1)
            mk = maskf4 if (first and qt == 0) else mask4
            mb = Bmaskf4 if (first and qt == 0) else Bmask4
            for pi, p2 in enumerate(pairs):
                for hp in range(2):
                    col = 512 * hp + 256 * pi
                    S.do("tensor", lambda e, sp=sp, col=col, hp=hp,
                         p2=p2, qs=qs: e.matmul(
                             psf[sp][:, col:col + 256],
                             q_fm[64 * hp:64 * (hp + 1), p2, qs:qs + 128],
                             k_fm[64 * hp:64 * (hp + 1), p2, qs:qs + 256],
                             start=True, stop=True,
                             tile_position=(64 * hp, 0)),
                         reads=[Bqfm, Bkfm], writes=[Bps[sp][hp]])
                for hp in range(2):
                    j = 2 * hp + pi
                    S.do("tensor", lambda e, sp=sp, j=j, mk=mk: e.matmul(
                        psf[sp][:, 256 * j:256 * (j + 1)], identb[:],
                        mk[:, 256 * j:256 * (j + 1)], start=False, stop=True,
                        skip_group_check=True),
                        reads=[mb, Bidentb], writes=[Bps[sp][j // 2]])

        def _att_s2(g, w):
            wp = w % NWP
            wb_ = w % NWPB
            sp = 1 + w % 2
            for j in range(4):
                S.do("scalar", lambda e, wp=wp, sp=sp, j=j: e.activation(
                    p32[wp][:, 256 * j:256 * (j + 1)],
                    psf[sp][:, 256 * j:256 * (j + 1)], AF.Exp,
                    bias=0.0, scale=1.0,
                    accum_out=den[wp][:, j:j + 1]),
                    reads=[Bps[sp][j // 2]],
                    writes=[Bp32[wp], Bden[wp]])
            S.do("vector", lambda e, wp=wp: e.reciprocal(
                rec[wp][:], den[wp][:]), reads=[Bden[wp]], writes=[Brec[wp]])
            for j in range(4):
                S.do("vector", lambda e, wp=wp, wb_=wb_, j=j: e.tensor_scalar_mul(
                    pbf[wb_][:, 256 * j:256 * (j + 1)],
                    p32[wp][:, 256 * j:256 * (j + 1)],
                    rec[wp][:, j:j + 1]),
                    reads=[Bp32[wp], Brec[wp]], writes=[Bpbf[wb_]])

        def _att_s3(g, w):
            wb_ = w % NWPB
            tp = w % 2
            for i in range(8):
                S.do("tensor", lambda e, wb_=wb_, tp=tp, i=i: e.transpose(
                    ps_tr[tp][:, 128 * i:128 * (i + 1)],
                    pbf[wb_][:, 128 * i:128 * (i + 1)], identb[:]),
                    reads=[Bpbf[wb_], Bidentb], writes=[Bpstr[tp]])
            S.do("vector", lambda e, wb_=wb_, tp=tp: e.tensor_copy(
                ptb[wb_][:], ps_tr[tp][:]), reads=[Bpstr[tp]],
                writes=[Bptb[wb_]])

        def _att_s4(g, w):
            wp = w % NWPB
            qt, ww = w // 4, w % 4
            qs = 128 * qt
            pairs = (2 * ww, 2 * ww + 1)
            for pi, p2 in enumerate(pairs):
                half = p2 // 4
                colq = 512 * half + 128 * (p2 % 4)
                for hp in range(2):
                    h = 2 * p2 + hp
                    j = 2 * hp + pi
                    for kb in range(2):
                        S.do("tensor", lambda e, wp=wp, hp=hp, h=h, j=j,
                             kb=kb, qt=qt, colq=colq: e.matmul(
                                 psf[0][64 * hp:64 * (hp + 1),
                                        colq:colq + 128],
                                 v_tm[:, qt + kb, 64 * h:64 * (h + 1)],
                                 ptb[wp][:, 256 * j + 128 * kb:
                                          256 * j + 128 * (kb + 1)],
                                 start=(kb == 0), stop=(kb == 1),
                                 tile_position=(0, 64 * hp)),
                             reads=[Bvtm, Bptb[wp]], writes=[Bps[0][half]])
            if ww in (1, 3):
                half = 0 if ww == 1 else 1
                S.do("scalar", lambda e, half=half, qs=qs, qt=qt: e.activation(
                    attn_fm[:, 4 * half:4 * (half + 1), qs:qs + 128],
                    psf[0][:, 512 * half:512 * (half + 1)],
                    AF.Identity, bias=0.0, scale=1.0),
                    reads=[Bps[0][half]], writes=[Battn[qt]])

        STAGGER = 1

        def emit_attention_all(g):
            for s in range(13):
                if s < 8:
                    _att_s1(g, s)
                if 1 <= s <= 8:
                    _att_s2(g, s - 1)
                if s in (1, 2) and g + 1 < ntt:
                    emit_xtr(g + 1, s - 1)
                if 3 <= s <= 10:
                    _att_s3(g, s - 3)
                if 5 <= s <= 12:
                    _att_s4(g, s - 5)

        def emit_wo_mm(g, jt):
            ps, Bp = grab()
            for nh in range(2):
                reg = ps[:, 512 * nh:512 * (nh + 1)]
                for kc in range(NKC):
                    S.do("tensor", lambda e, reg=reg, kc=kc, jt=jt, nh=nh:
                         e.matmul(reg,
                                  attn_fm[:, kc, 128 * jt:128 * (jt + 1)],
                                  wo_b[:, kc, 512 * nh:512 * (nh + 1)],
                                  start=(kc == 0), stop=False),
                         reads=[Battn[jt], Bw["wo"][kc]], writes=[Bp[nh]])
                S.do("tensor", lambda e, reg=reg, nh=nh: e.matmul(
                    reg, ones_t[:, 0:128],
                    brow_bf[0:1, 3072 + 512 * nh:3072 + 512 * (nh + 1)],
                    start=False, stop=True),
                    reads=[Bones, Bbrowbf], writes=[Bp[nh]])
            return ps, Bp

        def emit_ln1_y(g, jt, ps, Bp):
            par = g % 2
            _ln(S, nc, "vector", jt, par, ps, Bp, x_tm, Bxtm, z1, Bz1,
                y_t, By, zz, Bzz, sums, Bsums, sumsq, Bsumsq, mean_t, Bmean,
                msq, Bmsq, var_t, Bvar, lnv, Blnv, rs_t, Brs, sb_t, Bsb,
                eps_t, Beps, resid_src="x")
            S.do("vector", lambda e, jt=jt: e.tensor_copy(
                y_bf[:, 1024 * jt:1024 * (jt + 1)], y_t[:, jt, :]),
                reads=[By[jt]], writes=[Bybf[jt]])

        def emit_ytr(g, jt):
            for kc in range(NKC):
                S.do("tensor", lambda e, jt=jt, kc=kc: e.transpose(
                    ps_tr[jt][:, 128 * kc:128 * (kc + 1)],
                    y_bf[:, 1024 * jt + 128 * kc:1024 * jt + 128 * (kc + 1)],
                    identb[:]), reads=[Bybf[jt], Bidentb], writes=[Bpstr[jt]])
            S.do("vector", lambda e, jt=jt: e.tensor_copy(
                y_fm[:, :, 128 * jt:128 * (jt + 1)], ps_tr[jt][:]),
                reads=[Bpstr[jt]], writes=[Byfm])

        def emit_ffn_out(g, jt):
            b, tt = g // 8, g % 8
            t0 = tt * TT
            par = g % 2
            ps, Bp = grab()
            for nh in range(2):
                reg = ps[:, 512 * nh:512 * (nh + 1)]
                for kc in range(NKC):
                    S.do("tensor", lambda e, reg=reg, kc=kc, jt=jt, nh=nh:
                         e.matmul(reg,
                                  y_fm[:, kc, 128 * jt:128 * (jt + 1)],
                                  wfT_b[:, kc, 512 * nh:512 * (nh + 1)],
                                  start=(kc == 0), stop=False),
                         reads=[Byfm, Bw["wf"][kc]], writes=[Bp[nh]])
                S.do("tensor", lambda e, reg=reg, jt=jt, nh=nh: e.matmul(
                    reg, pool_b[0:64, 128 * jt:128 * (jt + 1)],
                    disha_b[0:64, 512 * nh:512 * (nh + 1)],
                    start=False, stop=True),
                    reads=[Bpool, Bdisha], writes=[Bp[nh]])
            _ln(S, nc, "vector", jt, par, ps, Bp, None, None, z1, Bz1,
                outb, Boutb, zz, Bzz, sums, Bsums, sumsq, Bsumsq,
                mean_t, Bmean, msq, Bmsq, var_t, Bvar, lnv, Blnv, rs_t, Brs,
                sb_t, Bsb, eps_t, Beps, resid_src="y", y_t=y_t, By=By)
            op = S.do("sync", lambda e, b=b, t0=t0, jt=jt:
                      e.dma_start(
                          out=out_d[b, t0 + 128 * jt:t0 + 128 * (jt + 1), :],
                          in_=outb[:, jt, :]),
                      reads=[Boutb[jt]], dma=True)
            out_dmas.append(op)

        emit_xcast(0)
        emit_xtr(0, 0)
        emit_xtr(0, 1)
        emit_xpipe(0)
        for ncx in range(NKC):
            st_t, st_b = stg[sidx % 2]
            sidx += 1
            S.do("sync", lambda e, ncx=ncx, st_t=st_t: e.dma_start(
                out=st_t[:], in_=w_d["wf"][128 * ncx:128 * (ncx + 1), :]),
                writes=[st_b], dma=True)
            S.do("vector", lambda e, st_t=st_t: e.tensor_copy(
                pbf[0][:], st_t[:]),
                 reads=[st_b], writes=[Bpbf[0]])
            for kc in range(NKC):
                S.do("tensor", lambda e, kc=kc: e.transpose(
                    ps_tr[0][:, 128 * kc:128 * (kc + 1)],
                    pbf[0][:, 128 * kc:128 * (kc + 1)], identb[:]),
                    reads=[Bpbf[0], Bidentb], writes=[Bpstr[0]])
            S.do("scalar", lambda e, ncx=ncx: e.activation(
                wfT_b[:, :, 128 * ncx:128 * (ncx + 1)], ps_tr[0][:],
                AF.Identity, bias=0.0, scale=1.0),
                reads=[Bpstr[0]], writes=[Bw["wf"][0]])


        for g in range(ntt):
            tt = g % 8
            if g + 1 < ntt:
                emit_xcast(g + 1)
            emit_attention_all(g)
            ps0, Bp0 = emit_wo_mm(g, 0)
            ps1, Bp1 = emit_wo_mm(g, 1)
            emit_ln1_y(g, 0, ps0, Bp0)
            emit_ln1_y(g, 1, ps1, Bp1)
            if taps.get("attn") == g:
                S.do("sync", lambda e: e.dma_start(out=tap_d["attn"],
                                                   in_=attn_fm[:]),
                     reads=[Battn[0], Battn[1]], dma=True)
            if taps.get("y") == g:
                S.do("sync", lambda e: e.dma_start(out=tap_d["y"], in_=y_t[:]),
                     reads=[By[0], By[1]], dma=True)
            if taps.get("z1") == g:
                S.do("sync", lambda e: e.dma_start(out=tap_d["z1"], in_=z1[:]),
                     reads=[Bz1[0], Bz1[1]], dma=True)
            if tt < 7:
                S.do("vector", lambda e: e.tensor_copy(
                    k_fm[:, :, 0:128], k_fm[:, :, 256:384]), writes=[Bkfm])
                S.do("vector", lambda e: e.tensor_copy(
                    v_tm[:, 0, :], v_tm[:, 2, :]), writes=[Bvtm])
            if g + 1 < ntt:
                emit_xpipe(g + 1)
            emit_ytr(g, 0)
            emit_ytr(g, 1)
            ps, Bp = grab()
            for kc in range(NKC):
                S.do("tensor", lambda e, kc=kc, ps=ps: e.matmul(
                    ps[0:64, 0:256], gmat_b[:], y_fm[:, kc, :],
                    start=(kc == 0), stop=(kc == NKC - 1)),
                    reads=[Bgmat, Byfm], writes=[Bp[0]])
            S.do("scalar", lambda e, ps=ps: e.activation(
                pool_b[:], ps[0:64, 0:256], AF.Identity, bias=0.0, scale=1.0),
                reads=[Bp[0]], writes=[Bpool])
            if taps.get("pool") == g:
                S.do("sync", lambda e: e.dma_start(out=tap_d["pool"],
                                                   in_=pool_b[:]),
                     reads=[Bpool], dma=True)
            emit_ffn_out(g, 0)
            emit_ffn_out(g, 1)
        S.do("sync", None, deps=out_dmas)

        S.finalize(nc, es, paranoid, par_eng)
    return nc


def _ln(S, nc, veng, jt, par, ps_bg, Bbg, x_tm, Bxtm, z1, Bz1, dst, Bdst,
        zz, Bzz, sums, Bsums, sumsq, Bsumsq, mean_t, Bmean, msq, Bmsq,
        var_t, Bvar, lnv, Blnv, rs_t, Brs, sb_t, Bsb, eps_t, Beps,
        resid_src="x", y_t=None, By=None):
    if resid_src == "x":
        resid = x_tm[par][:, jt, :]
        Bres = Bxtm[par][jt]
    else:
        resid = y_t[:, jt, :]
        Bres = By[jt]
    S.do("vector", lambda e: e.scalar_tensor_tensor(
        z1[:, jt, :], ps_bg[:], 1.0, resid, ALU.mult, ALU.add,
        accum_out=sums[:, jt:jt + 1]),
        reads=[Bbg[0], Bbg[1], Bres], writes=[Bz1[jt], Bsums])
    S.do("vector", lambda e: e.scalar_tensor_tensor(
        zz[:], z1[:, jt, :], 1.0, z1[:, jt, :], ALU.mult, ALU.mult,
        accum_out=sumsq[:, jt:jt + 1]),
        reads=[Bz1[jt]], writes=[Bzz, Bsumsq])
    S.do("vector", lambda e: e.tensor_scalar_mul(
        mean_t[:, jt:jt + 1], sums[:, jt:jt + 1], 1.0 / C),
        reads=[Bsums], writes=[Bmean])
    S.do("vector", lambda e: e.tensor_mul(
        msq[:, jt:jt + 1], mean_t[:, jt:jt + 1], mean_t[:, jt:jt + 1]),
        reads=[Bmean], writes=[Bmsq])
    S.do("vector", lambda e: e.scalar_tensor_tensor(
        var_t[:, jt:jt + 1], sumsq[:, jt:jt + 1], 1.0 / C, msq[:, jt:jt + 1],
        ALU.mult, ALU.subtract),
        reads=[Bsumsq, Bmsq], writes=[Bvar])
    S.do("scalar", lambda e: e.activation(
        lnv[:, jt:jt + 1], var_t[:, jt:jt + 1], AF.Ln, bias=eps_t[:], scale=1.0),
        reads=[Bvar, Beps], writes=[Blnv])
    S.do("scalar", lambda e: e.activation(
        rs_t[:, jt:jt + 1], lnv[:, jt:jt + 1], AF.Exp, bias=0.0, scale=-0.5),
        reads=[Blnv], writes=[Brs])
    S.do("vector", lambda e: e.tensor_scalar(
        sb_t[:, jt:jt + 1], mean_t[:, jt:jt + 1], rs_t[:, jt:jt + 1], -1.0,
        ALU.mult, ALU.mult),
        reads=[Bmean, Brs], writes=[Bsb])
    S.do("scalar", lambda e: e.activation(
        dst[:, jt, :], z1[:, jt, :], AF.Identity,
        bias=sb_t[:, jt:jt + 1], scale=rs_t[:, jt:jt + 1]),
        reads=[Bz1[jt], Brs, Bsb], writes=[Bdst[jt]])


_NC_CACHE = {}


def _get_nc(taps_key=None):
    if taps_key not in _NC_CACHE:
        _NC_CACHE[taps_key] = build_program(dict(taps_key) if taps_key else None)
    return _NC_CACHE[taps_key]


def kernel(L0_attn_x, L0_attn_Wq, L0_attn_bq, L0_attn_Wk, L0_attn_bk,
           L0_attn_Wv, L0_attn_bv, L0_attn_Wo, L0_attn_bo,
           L0_ffn_weight, L0_ffn_disha):
    x = np.ascontiguousarray(np.asarray(L0_attn_x))
    biases = np.ascontiguousarray(np.stack([
        np.asarray(L0_attn_bq), np.asarray(L0_attn_bk),
        np.asarray(L0_attn_bv), np.asarray(L0_attn_bo)]).reshape(1, 4 * 1024))
    biasT = np.ascontiguousarray(np.stack(
        [np.asarray(b).reshape(8, 128).T for b in
         (L0_attn_bq, L0_attn_bk, L0_attn_bv, L0_attn_bo)],
        axis=1).reshape(128, 32))
    common = {
        "biasT": biasT,
        "wq": np.ascontiguousarray(np.asarray(L0_attn_Wq)),
        "wk": np.ascontiguousarray(np.asarray(L0_attn_Wk)),
        "wv": np.ascontiguousarray(np.asarray(L0_attn_Wv)),
        "wo": np.ascontiguousarray(np.asarray(L0_attn_Wo)),
        "wf": np.ascontiguousarray(np.asarray(L0_ffn_weight)),
        "disha": np.ascontiguousarray(np.asarray(L0_ffn_disha)),
        "bias": biases,
    }
    in_maps = [dict(common, x=np.ascontiguousarray(x[2 * c:2 * c + 2]))
               for c in range(8)]
    nc = _get_nc()
    res = run_bass_kernel_spmd(nc, in_maps, core_ids=list(range(8)))
    out = np.concatenate([res.results[c]["out"] for c in range(8)], axis=0)
    return out.astype(np.float32, copy=False)
